# version 05
# baseline (speedup 1.0000x reference)
import numpy as np
import concourse.bass as bass
import concourse.mybir as mybir
from concourse.bass_utils import run_bass_kernel_spmd

F32 = mybir.dt.float32
BF = mybir.dt.bfloat16
AF = mybir.ActivationFunctionType
ALU = mybir.AluOpType
AXX = mybir.AxisListType.X
MAGIC = 12582912.0
INV2PI = float(1.0 / (2.0 * np.pi))
TWOPI = float(2.0 * np.pi)

T_TOK, DM, P, E, H = 2048, 1024, 512, 8, 4096
NTT, NDC, NPC = 16, 8, 4


class Ctr:
    def __init__(self, name):
        self.name, self.n, self.h = name, 0, None


class Plan:
    def __init__(self):
        self.streams = {"tensor": [], "vector": [], "scalar": [], "sync": [], "gpsimd": []}

    def emit(self, eng, fn):
        self.streams[eng].append(fn)

    def wait(self, eng, ctr, val):
        if val <= 0:
            return
        assert val <= ctr.n, f"{eng} waits {ctr.name}>={val}, only {ctr.n} planned"
        self.emit(eng, lambda e, c=ctr, v=val: e.wait_ge(c.h, v))

    def op(self, eng, fn, inc=None, k=1):
        if inc is None:
            self.emit(eng, fn)
            return None
        inc.n += k
        self.emit(eng, lambda e, fn=fn, c=inc, k=k: fn(e).then_inc(c.h, k))
        return inc.n


def build_program(debug=False, eorder=None, dram_perm=None, p1q=True):
    if eorder is None:
        eorder = list(range(8))
    if dram_perm is None:
        dram_perm = list(range(8))
    nc = bass.Bass("TRN2", target_bir_lowering=False)

    u_d = nc.dram_tensor("u", [T_TOK, DM], F32, kind="ExternalInput").ap()
    lre_d = nc.dram_tensor("lre", [P], F32, kind="ExternalInput").ap()
    lim_d = nc.dram_tensor("lim", [P], F32, kind="ExternalInput").ap()
    lstep_d = nc.dram_tensor("lstep", [P, 1], F32, kind="ExternalInput").ap()
    b_d = nc.dram_tensor("bmat", [P, DM, 2], F32, kind="ExternalInput").ap()
    c_d = nc.dram_tensor("cmat", [DM, P, 2], F32, kind="ExternalInput").ap()
    dv_d = nc.dram_tensor("dvec", [DM], F32, kind="ExternalInput").ap()
    w1_d = nc.dram_tensor("w1", [E, DM, H], F32, kind="ExternalInput").ap()
    b1_d = nc.dram_tensor("b1", [E, H], F32, kind="ExternalInput").ap()
    w2_d = nc.dram_tensor("w2", [E, H, DM], F32, kind="ExternalInput").ap()
    b2_d = nc.dram_tensor("b2", [E, DM], F32, kind="ExternalInput").ap()
    wr_d = nc.dram_tensor("wr", [DM, E], F32, kind="ExternalInput").ap()
    br_d = nc.dram_tensor("br", [E], F32, kind="ExternalInput").ap()
    out_d = nc.dram_tensor("out", [T_TOK, DM], F32, kind="ExternalOutput").ap()
    if debug:
        dbg_w = nc.dram_tensor("dbg_w", [128, 128], F32, kind="ExternalOutput").ap()
        dbg_acc = nc.dram_tensor("dbg_acc", [T_TOK, DM], F32, kind="ExternalOutput").ap()
        dbg_xn = nc.dram_tensor("dbg_xn", [128, 16384], BF, kind="ExternalOutput").ap()
        dbg_xs = nc.dram_tensor("dbg_xs", [128, 16384], BF, kind="ExternalOutput").ap()
        dbg_bt = nc.dram_tensor("dbg_bt", [128, 8192], BF, kind="ExternalOutput").ap()
        dbg_cs = nc.dram_tensor("dbg_cs", [128, 4096], BF, kind="ExternalOutput").ap()
        dbg_pv = nc.dram_tensor("dbg_pv", [128, 64], F32, kind="ExternalOutput").ap()
        dbg_pv0 = nc.dram_tensor("dbg_pv0", [128, 64], F32, kind="ExternalOutput").ap()
        dbg_ct = nc.dram_tensor("dbg_ct", [128, 8192], BF, kind="ExternalOutput").ap()
        dbg_csc = nc.dram_tensor("dbg_csc", [128, 3072], F32, kind="ExternalOutput").ap()
        dbg_id = nc.dram_tensor("dbg_id", [128, 128], F32, kind="ExternalOutput").ap()

    plan = Plan()
    CT, CV, CS, CG = Ctr("T"), Ctr("V"), Ctr("S"), Ctr("G")
    CU, CW, CF, CST = Ctr("U"), Ctr("W"), Ctr("F"), Ctr("ST")
    ctrs = [CT, CV, CS, CG, CU, CW, CF, CST]

    NFS = 4608
    NF = 16384 + 128 + 256 + 64 + 128 + 128 + 8 + 1 + 64 + 64 + 32 + 2048 + NFS
    NB = 16384 + 36864 + 4096
    fA = nc.sbuf_tensor("fA", [128, NF], F32).__enter__()
    bA = nc.sbuf_tensor("bA", [128, NB], BF).__enter__()

    off = [0]
    def takeF(cols, parts=128):
        ap = fA[0:parts, off[0]:off[0] + cols]
        off[0] += cols
        assert off[0] <= NF
        return ap

    out_acc = takeF(16384)
    w_rt = takeF(128)
    b1c = takeF(256)
    wr_sb = takeF(64)
    ident = takeF(128)
    ones1 = takeF(128, parts=1)
    br_row = takeF(8, parts=1)
    epsc = takeF(1)
    pv = takeF(64)
    sm = takeF(64)
    stats = takeF(32)
    w_em = takeF(2048, parts=8)
    SCRF = off[0]
    takeF(NFS)

    def fsl(o, cols, parts=128):
        assert o + cols <= NFS
        return fA[0:parts, SCRF + o: SCRF + o + cols]

    xn_bf = bA[:, 0:16384]
    SCRB = 16384
    def bsl(o, cols):
        assert o + cols <= 36864 + 4096
        return bA[:, SCRB + o: SCRB + o + cols]

    RHO, THP, SGR, SGI, SGIN, T0, T1, T2, T3, T4, T5, T6 = range(12)
    def pva(q):
        return pv[:, q::16]
    def pvc(pc, q):
        return pv[:, pc * 16 + q: pc * 16 + q + 1]

    ps = []
    for i in range(8):
        ps.append(nc.psum_tensor(f"ps{i}", [128, 512], F32).__enter__())
    for c in ctrs:
        c.h = nc.semaphore(c.name).__enter__()
    CSTD = Ctr("STD")
    CSTD.h = nc.semaphore("STD").__enter__()

    W, OP = plan.wait, plan.op

    def vf():
        OP("vector", lambda v: v.engine_nop(), inc=CF)
        W("vector", CF, CF.n)

    def vpad():
        OP("vector", lambda v: v.tensor_copy(pv[:, 15:16], pv[:, 14:15]), inc=CV)
        return CV.n

    def spad():
        OP("scalar", lambda s: s.activation(pv[:, 31:32], pv[:, 31:32], AF.Copy), inc=CS)
        return CS.n

    def dma(eng, out_ap, in_ap, ctr):
        return OP(eng, lambda e, o=out_ap, i=in_ap: e.dma_start(out=o, in_=i), inc=ctr, k=16)

    lrel4 = fsl(0, 128, parts=4)
    liml4 = fsl(128, 128, parts=4)
    lsl4 = fsl(256, 128, parts=4)
    iotad = fsl(384, 128)
    b1st = [fsl(512, 128, parts=32), fsl(640, 128, parts=32)]

    u_l0 = dma("sync", lrel4, lre_d.rearrange("(c p) -> c p", p=128), CU)
    dma("sync", liml4, lim_d.rearrange("(c p) -> c p", p=128), CU)
    u_l2 = dma("sync", lsl4, lstep_d.rearrange("(c p) o -> c (p o)", p=128), CU)
    u_br = dma("sync", br_row, br_d.rearrange("(o n) -> o n", o=1), CU)
    for c in range(NDC):
        u_wr = dma("sync", wr_sb[:, c * 8:(c + 1) * 8],
                   wr_d.rearrange("(c p) e -> c p e", p=128)[c], CU)

    OP("gpsimd", lambda g: g.iota(iotad, [[1, 128]], channel_multiplier=-1,
                                  allow_small_or_imprecise_dtypes=True), inc=CG)
    rampr = fsl(768, 128, parts=1)
    rowc = fsl(896, 1)
    W("vector", CU, u_l0)
    OP("vector", lambda v: v.tensor_scalar(ones1[:], lrel4[0:1, 0:128], 0.0, 1.0, ALU.mult, ALU.add), inc=CV)
    vf()
    OP("vector", lambda v: v.tensor_tensor_scan(rampr, ones1[:], ones1[:], 0.0, ALU.mult, ALU.add), inc=CV)
    vf()
    v_ramp = vpad()
    W("sync", CV, v_ramp)
    u_rowc = dma("sync", rowc, rampr, CU)
    W("tensor", CV, v_ramp)
    t_bc = OP("tensor", lambda t: t.matmul(ps[6][:, 384:512], ones1[:], rampr, start=True, stop=True), inc=CT)
    W("vector", CT, t_bc)
    W("vector", CU, u_rowc)
    OP("vector", lambda v: v.tensor_scalar(ident[:], ps[6][:, 384:512], rowc[:, 0:1], None, ALU.is_equal), inc=CV)
    OP("vector", lambda v: v.tensor_scalar(epsc[:], rowc[:, 0:1], 0.0, 1e-5, ALU.mult, ALU.add), inc=CV)
    vf()
    v_const = vpad()

    W("tensor", CU, u_l2)
    W("tensor", CV, v_const)
    t3 = []
    for i, src in enumerate([lrel4, liml4, lsl4]):
        t3.append(OP("tensor", lambda t, s=src, i=i:
                     t.transpose(ps[6][:, i * 4:(i + 1) * 4], s, ident[0:4, 0:4]), inc=CT))
    W("vector", CT, t3[-1])
    for i, q in enumerate([T0, T1, T2]):
        OP("vector", lambda v, i=i, q=q: v.tensor_copy(pva(q), ps[6][:, i * 4:(i + 1) * 4]), inc=CV)
    vf()
    v_pv = vpad()
    W("scalar", CV, v_pv)
    s_ev = OP("scalar", lambda s: s.activation(pva(T2), pva(T2), AF.Exp), inc=CS)
    spad()
    W("vector", CS, s_ev + 1)
    OP("vector", lambda v: v.tensor_scalar(pva(T0), pva(T0), -1e-4, None, ALU.min), inc=CV); vf()
    OP("vector", lambda v: v.tensor_mul(pva(T3), pva(T0), pva(T2)), inc=CV); vf()
    OP("vector", lambda v: v.tensor_mul(pva(T4), pva(T1), pva(T2)), inc=CV); vf()
    OP("vector", lambda v: v.tensor_scalar(pva(THP), pva(T4), INV2PI, None, ALU.mult), inc=CV); vf()
    v_thp = vpad()
    W("scalar", CV, v_thp)
    OP("scalar", lambda s: s.activation(pva(RHO), pva(T3), AF.Exp), inc=CS)

    def frac4(src_q, dst_q, tmp_q, plus_quarter):
        base = src_q
        if plus_quarter:
            OP("vector", lambda v: v.tensor_scalar(pva(tmp_q), pva(src_q), 0.25, None, ALU.add), inc=CV); vf()
            base = tmp_q
        OP("vector", lambda v, b=base, d=dst_q: v.tensor_scalar(pva(d), pva(b), MAGIC, None, ALU.add), inc=CV); vf()
        OP("vector", lambda v, d=dst_q: v.tensor_scalar(pva(d), pva(d), -MAGIC, None, ALU.add), inc=CV); vf()
        OP("vector", lambda v, b=base, d=dst_q: v.tensor_sub(pva(d), pva(b), pva(d)), inc=CV); vf()

    frac4(THP, T5, T6, False)
    frac4(THP, T2, T6, True)
    v_frac = vpad()
    W("scalar", CV, v_frac)
    OP("scalar", lambda s: s.activation(pva(T5), pva(T5), AF.Sin, scale=TWOPI), inc=CS)
    OP("scalar", lambda s: s.activation(pva(T2), pva(T2), AF.Sin, scale=TWOPI), inc=CS)
    s_trig = spad()
    W("vector", CS, s_trig)
    OP("vector", lambda v: v.tensor_mul(pva(T5), pva(RHO), pva(T5)), inc=CV); vf()
    OP("vector", lambda v: v.tensor_mul(pva(T2), pva(RHO), pva(T2)), inc=CV); vf()
    OP("vector", lambda v: v.tensor_scalar(pva(T2), pva(T2), -1.0, None, ALU.add), inc=CV); vf()
    OP("vector", lambda v: v.tensor_mul(pva(T6), pva(T0), pva(T0)), inc=CV); vf()
    OP("vector", lambda v: v.tensor_mul(pva(T3), pva(T1), pva(T1)), inc=CV); vf()
    OP("vector", lambda v: v.tensor_add(pva(T6), pva(T6), pva(T3)), inc=CV); vf()
    OP("vector", lambda v: v.reciprocal(pva(T6), pva(T6)), inc=CV); vf()
    OP("vector", lambda v: v.tensor_mul(pva(T3), pva(T2), pva(T0)), inc=CV); vf()
    OP("vector", lambda v: v.tensor_mul(pva(T4), pva(T5), pva(T1)), inc=CV); vf()
    OP("vector", lambda v: v.tensor_add(pva(T3), pva(T3), pva(T4)), inc=CV); vf()
    OP("vector", lambda v: v.tensor_mul(pva(SGR), pva(T3), pva(T6)), inc=CV); vf()
    OP("vector", lambda v: v.tensor_mul(pva(T3), pva(T5), pva(T0)), inc=CV); vf()
    OP("vector", lambda v: v.tensor_mul(pva(T4), pva(T2), pva(T1)), inc=CV); vf()
    OP("vector", lambda v: v.tensor_sub(pva(T3), pva(T3), pva(T4)), inc=CV); vf()
    OP("vector", lambda v: v.tensor_mul(pva(SGI), pva(T3), pva(T6)), inc=CV); vf()
    OP("vector", lambda v: v.tensor_scalar(pva(SGIN), pva(SGI), -1.0, None, ALU.mult), inc=CV)
    v_sigma = CV.n
    vf()

    if debug:
        W("sync", CV, v_sigma)
        dma("sync", dbg_pv0, pv, CSTD)
        dma("sync", dbg_id, ident, CSTD)

    t_b1 = []
    v_b1 = []
    for e in range(E):
        par = e % 2
        if e >= 2:
            W("sync", CV, v_b1[e - 2])
        u_b1 = dma("sync", b1st[par], b1_d[e].rearrange("(r p) -> r p", p=128), CU)
        W("tensor", CU, u_b1)
        if e >= 1:
            W("tensor", CV, v_b1[e - 1])
        t_b1.append(OP("tensor", lambda t, par=par: t.transpose(ps[6][:, 128:160], b1st[par], ident[0:32, 0:32]), inc=CT))
        W("vector", CT, t_b1[-1])
        v_b1.append(OP("vector", lambda v, e=e: v.tensor_copy(b1c[:, e * 32:(e + 1) * 32], ps[6][:, 128:160]), inc=CV))

    ub = [fsl(0, 1024), fsl(1024, 1024)]
    xnf = [fsl(2048, 1024), fsl(3072, 1024)]
    sqd = xnf[0]
    lg_all = fsl(4096, 128)
    eq_all = fsl(4224, 128)
    elg_all = fsl(4352, 128)
    rs16 = fsl(4480, 96)

    v_red = [0] * NTT
    s_sq = [0] * NTT
    for tt in range(NTT):
        par = tt % 2
        if tt == 0:
            W("sync", CT, t_b1[-1])
            W("sync", CV, v_b1[-1])
        q = "sync"
        qc = CU
        if tt >= 2:
            W(q, CV, v_red[tt - 2])
            W(q, CS, s_sq[tt - 2])
        u_ev = dma(q, ub[par], u_d[tt * 128:(tt + 1) * 128, :], qc)
        W("vector", qc, u_ev)
        v_red[tt] = OP("vector", lambda v, par=par, tt=tt: v.reduce_sum(sm[:, tt:tt + 1], ub[par], axis=AXX), inc=CV)
        W("scalar", qc, u_ev)
        s_sq[tt] = OP("scalar", lambda s, par=par, tt=tt: s.activation(sqd, ub[par], AF.Square,
                                                                      accum_out=sm[:, 16 + tt:16 + tt + 1]), inc=CS)
    W("vector", CS, s_sq[-1])
    vf()
    OP("vector", lambda v: v.tensor_scalar(rs16[:, 80:96], sm[:, 0:16], 1.0 / DM, None, ALU.mult), inc=CV)
    vf()
    OP("vector", lambda v: v.tensor_mul(sm[:, 32:48], rs16[:, 80:96], rs16[:, 80:96]), inc=CV)
    vf()
    OP("vector", lambda v: v.scalar_tensor_tensor(sm[:, 48:64], sm[:, 16:32], 1.0 / DM, sm[:, 32:48],
                                                  ALU.mult, ALU.subtract), inc=CV)
    vf()
    OP("vector", lambda v: v.tensor_scalar(stats[:, 0::2], rs16[:, 80:96], -1.0, None, ALU.mult), inc=CV)
    vf()
    v_nm = vpad()
    W("scalar", CV, v_nm)
    OP("scalar", lambda s: s.activation(sm[:, 48:64], sm[:, 48:64], AF.Ln, bias=epsc[:, 0:1], scale=1.0), inc=CS)
    OP("scalar", lambda s: s.activation(stats[:, 1::2], sm[:, 48:64], AF.Exp, scale=-0.5), inc=CS)
    s_rstd = spad()
    W("vector", CS, s_rstd)
    OP("vector", lambda v: v.tensor_mul(sm[:, 0:16], stats[:, 0::2], stats[:, 1::2]), inc=CV)
    v_nmr = CV.n
    vf()

    v_xn_ev = [0] * NTT
    s_xn_ev = [0] * NTT
    v_lg_ev = [0] * NTT
    t_tr_last = [0] * NTT
    bank01_free = [(CV, 0), (CV, 0)]
    v_p1c = [0] * NTT
    for tt in range(NTT):
        par = tt % 2
        q = "sync"
        qc = CU
        if tt < 2:
            W(q, CV, v_red[NTT - 2 + tt])
            W(q, CS, s_sq[NTT - 2 + tt])
        else:
            W(q, CV, v_p1c[tt - 2])
            W(q, CT, t_tr_last[tt - 2])
        u_ev = dma(q, ub[par], u_d[tt * 128:(tt + 1) * 128, :], qc)
        W("vector", qc, u_ev)
        if tt == 0:
            W("vector", CS, s_rstd)
        OP("vector", lambda v, par=par, tt=tt: v.tensor_scalar(ub[par], ub[par],
                                                               stats[:, 2 * tt:2 * tt + 1],
                                                               stats[:, 2 * tt + 1:2 * tt + 2],
                                                               ALU.add, ALU.mult), inc=CV)
        v_xn_tm = CV.n
        v_p1c[tt] = v_xn_tm

        W("tensor", CV, v_xn_tm)
        for dc in range(NDC):
            bank = dc % 2
            gctr, gval = bank01_free[bank]
            W("tensor", gctr, gval)
            OP("tensor", lambda t, dc=dc, par=par, bank=bank:
               t.transpose(ps[bank][:, 0:128], ub[par][:, dc * 128:(dc + 1) * 128], ident[:]), inc=CT)
            t_ev = CT.n
            if bank == 1:
                W("vector", CT, t_ev)
                OP("vector", lambda v, dc=dc, par=par, bank=bank:
                   v.tensor_copy(xnf[par][:, dc * 128:(dc + 1) * 128], ps[bank][:, 0:128]), inc=CV)
                OP("vector", lambda v, dc=dc, tt=tt, bank=bank:
                   v.tensor_copy(xn_bf[:, dc * 2048 + tt * 128: dc * 2048 + (tt + 1) * 128], ps[bank][:, 0:128]), inc=CV)
                bank01_free[bank] = (CV, CV.n)
            else:
                W("scalar", CT, t_ev)
                OP("scalar", lambda s, dc=dc, par=par, bank=bank:
                   s.copy(xnf[par][:, dc * 128:(dc + 1) * 128], ps[bank][:, 0:128]), inc=CS)
                OP("scalar", lambda s, dc=dc, tt=tt, bank=bank:
                   s.copy(xn_bf[:, dc * 2048 + tt * 128: dc * 2048 + (tt + 1) * 128], ps[bank][:, 0:128]), inc=CS)
                bank01_free[bank] = (CS, CS.n)
        t_tr_last[tt] = CT.n
        v_xn_ev[tt] = CV.n
        s_xn_ev[tt] = CS.n

        W("tensor", CV, v_xn_ev[tt])
        W("tensor", CS, s_xn_ev[tt])
        if tt > 0:
            W("tensor", CV, v_lg_ev[tt - 1])
        W("tensor", CU, u_br)
        for dc in range(NDC):
            OP("tensor", lambda t, dc=dc, par=par:
               t.matmul(ps[2][:, 0:8], xnf[par][:, dc * 128:(dc + 1) * 128],
                        wr_sb[:, dc * 8:(dc + 1) * 8], start=(dc == 0), stop=False), inc=CT)
        t_rt = OP("tensor", lambda t: t.matmul(ps[2][:, 0:8], ones1[:], br_row[:],
                                               start=False, stop=True), inc=CT)
        W("vector", CT, t_rt)
        v_lg_ev[tt] = OP("vector", lambda v, tt=tt: v.tensor_copy(lg_all[:, tt * 8:(tt + 1) * 8], ps[2][:, 0:8]), inc=CV)

    vf()
    for tt in range(NTT):
        OP("vector", lambda v, tt=tt: v.reduce_max(rs16[:, tt:tt + 1], lg_all[:, tt * 8:(tt + 1) * 8], axis=AXX), inc=CV)
    vf()
    OP("vector", lambda v: v.tensor_scalar(rs16[:, 32:48], rs16[:, 0:16], -1.0, None, ALU.mult), inc=CV)
    for tt in range(NTT):
        OP("vector", lambda v, tt=tt: v.tensor_scalar(eq_all[:, tt * 8:(tt + 1) * 8], lg_all[:, tt * 8:(tt + 1) * 8],
                                                      rs16[:, tt:tt + 1], None, ALU.is_equal), inc=CV)
    vf()
    for tt in range(NTT):
        OP("vector", lambda v, tt=tt: v.scalar_tensor_tensor(eq_all[:, tt * 8:(tt + 1) * 8], eq_all[:, tt * 8:(tt + 1) * 8],
                                                             -1e30, lg_all[:, tt * 8:(tt + 1) * 8], ALU.mult, ALU.add), inc=CV)
    vf()
    for tt in range(NTT):
        OP("vector", lambda v, tt=tt: v.reduce_max(rs16[:, 16 + tt:16 + tt + 1], eq_all[:, tt * 8:(tt + 1) * 8], axis=AXX), inc=CV)
    v_m2a = CV.n
    vf()
    W("scalar", CV, v_m2a)
    for tt in range(NTT):
        OP("scalar", lambda s, tt=tt: s.activation(elg_all[:, tt * 8:(tt + 1) * 8], lg_all[:, tt * 8:(tt + 1) * 8],
                                                   AF.Exp, bias=rs16[:, 32 + tt:32 + tt + 1], scale=1.0), inc=CS)
    s_elg = spad()
    W("vector", CS, s_elg)
    for tt in range(NTT):
        OP("vector", lambda v, tt=tt: v.tensor_scalar(eq_all[:, tt * 8:(tt + 1) * 8], lg_all[:, tt * 8:(tt + 1) * 8],
                                                      rs16[:, 16 + tt:16 + tt + 1], None, ALU.is_ge), inc=CV)
    vf()
    OP("vector", lambda v: v.tensor_mul(eq_all[:], elg_all[:], eq_all[:]), inc=CV)
    vf()
    for tt in range(NTT):
        OP("vector", lambda v, tt=tt: v.reduce_sum(rs16[:, 48 + tt:48 + tt + 1], eq_all[:, tt * 8:(tt + 1) * 8], axis=AXX), inc=CV)
    vf()
    OP("vector", lambda v: v.reciprocal(rs16[:, 64:80], rs16[:, 48:64]), inc=CV)
    vf()
    for tt in range(NTT):
        OP("vector", lambda v, tt=tt: v.tensor_scalar(w_rt[:, tt * 8:(tt + 1) * 8], eq_all[:, tt * 8:(tt + 1) * 8],
                                                      rs16[:, 64 + tt:64 + tt + 1], None, ALU.mult), inc=CV)
    v_w_all = CV.n
    vf()
    W("tensor", CV, v_w_all)
    wem_free = [(CV, v_b1[-1]), (CV, 0)]
    v_wem = [0] * NTT
    for tt in range(NTT):
        bank = 6 + tt % 2
        gctr, gval = wem_free[tt % 2]
        W("tensor", gctr, gval)
        t_w = OP("tensor", lambda t, tt=tt, bank=bank: t.transpose(ps[bank][0:8, 256:384], w_rt[:, tt * 8:(tt + 1) * 8], ident[:]), inc=CT)
        W("vector", CT, t_w)
        v_wem[tt] = OP("vector", lambda v, tt=tt, bank=bank: v.tensor_copy(w_em[0:8, tt * 128:(tt + 1) * 128], ps[bank][0:8, 256:384]), inc=CV)
        wem_free[tt % 2] = (CV, v_wem[tt])

    v_rt_ev = [0] * NTT
    v_rt_ev[NTT - 1] = v_lg_ev[NTT - 1]
    v_phase1 = CV.n
    t_phase1 = CT.n
    s_phase1 = CS.n

    xs_pl = [[bsl((comp * 4 + pc) * 2048, 2048) for pc in range(NPC)] for comp in range(2)]
    bt_pl = [[bsl(16384 + (comp * 8 + dc) * 512, 512) for dc in range(NDC)] for comp in range(2)]
    ct_pl = [[bsl(16384 + (pc * 2 + comp) * 1024, 1024) for comp in range(2)] for pc in range(NPC)]
    csn_c = bsl(24576, 2048)
    csn_s = bsl(26624, 2048)
    btw = [bsl(28672, 2048), bsl(30720, 2048)]

    braw = [fsl(0, 1024), fsl(1024, 1024)]
    bsc = fsl(2048, 1024)
    v_braw_done = [0, 0]
    t_bsc_done = 0
    v_2a_evac = [0, 0]
    s_2a_evac = [0, 0]
    W("tensor", CS, s_phase1)
    half = 0
    for pc in range(NPC):
        for dh in range(2):
            par = half % 2
            q = "sync"
            qc = CU
            if half < 2:
                W(q, CV, v_phase1)
                W(q, CT, t_phase1)
                W(q, CS, s_phase1)
            else:
                W(q, CV, v_braw_done[par])
            u_b = dma(q, braw[par],
                      b_d[pc * 128:(pc + 1) * 128, dh * 512:(dh + 1) * 512, :].rearrange("p h c -> p (h c)"), qc)
            W("vector", qc, u_b)
            if half == 0:
                W("vector", CV, v_sigma)
            W("vector", CT, t_bsc_done)
            OP("vector", lambda v, par=par, pc=pc: v.tensor_scalar(bsc[:, 0:512], braw[par][:, 0::2], pvc(pc, SGR), None, ALU.mult), inc=CV)
            OP("vector", lambda v, par=par, pc=pc: v.scalar_tensor_tensor(bsc[:, 0:512], braw[par][:, 1::2], pvc(pc, SGIN), bsc[:, 0:512], ALU.mult, ALU.add), inc=CV)
            OP("vector", lambda v, par=par, pc=pc: v.tensor_scalar(bsc[:, 512:1024], braw[par][:, 1::2], pvc(pc, SGR), None, ALU.mult), inc=CV)
            OP("vector", lambda v, par=par, pc=pc: v.scalar_tensor_tensor(bsc[:, 512:1024], braw[par][:, 0::2], pvc(pc, SGI), bsc[:, 512:1024], ALU.mult, ALU.add), inc=CV)
            v_braw_done[par] = CV.n
            v_bsc = CV.n
            W("tensor", CV, v_bsc)
            for comp in range(2):
                for hcq in range(4):
                    hc = dh * 4 + hcq
                    bank = (comp * 4 + hcq) % 2
                    if bank == 1:
                        gc, gv = CV, v_2a_evac[1]
                    else:
                        gc, gv = CS, s_2a_evac[0]
                    W("tensor", gc, gv)
                    OP("tensor", lambda t, comp=comp, hcq=hcq, bank=bank:
                       t.transpose(ps[bank][:, 0:128], bsc[:, comp * 512 + hcq * 128: comp * 512 + (hcq + 1) * 128], ident[:]), inc=CT)
                    t_ev = CT.n
                    if bank == 1:
                        W("vector", CT, t_ev)
                        OP("vector", lambda v, comp=comp, hc=hc, pc=pc, bank=bank:
                           v.tensor_copy(bt_pl[comp][hc][:, pc * 128:(pc + 1) * 128], ps[bank][:, 0:128]), inc=CV)
                        v_2a_evac[1] = CV.n
                    else:
                        W("scalar", CT, t_ev)
                        OP("scalar", lambda s, comp=comp, hc=hc, pc=pc, bank=bank:
                           s.copy(bt_pl[comp][hc][:, pc * 128:(pc + 1) * 128], ps[bank][:, 0:128]), inc=CS)
                        s_2a_evac[0] = CS.n
            t_bsc_done = CT.n
            half += 1
    v_2a_done = CV.n
    s_2a_done = CS.n

    mpl = fsl(0, 2048)
    rpl = fsl(2048, 2048)
    bu_free = {2: v_lg_ev[NTT - 1], 3: 0, 4: 0, 5: 0}
    v_scr_free = v_2a_done
    t_bu_all = 0
    for pc in range(NPC):
        if pc == 0:
            W("vector", CS, s_2a_done)
            W("vector", CT, t_bsc_done)
        OP("vector", lambda v: v.tensor_scalar(rpl, bt_pl[0][0].broadcast_to([128, 2048]) if False else bA[:, SCRB + 16384: SCRB + 18432], 0.0, 1.0, ALU.mult, ALU.add), inc=CV)
        vf()
        OP("vector", lambda v: v.tensor_tensor_scan(mpl, rpl, rpl, 0.0, ALU.mult, ALU.add), inc=CV)
        vf()
        OP("vector", lambda v, pc=pc: v.tensor_scalar(mpl, mpl, -1.0, pvc(pc, THP), ALU.add, ALU.mult), inc=CV)
        OP("vector", lambda v: v.tensor_scalar(rpl, mpl, MAGIC, None, ALU.add), inc=CV)
        OP("vector", lambda v: v.tensor_scalar(rpl, rpl, -MAGIC, None, ALU.add), inc=CV)
        OP("vector", lambda v: v.tensor_sub(rpl, mpl, rpl), inc=CV)
        v_fs = CV.n
        W("scalar", CV, v_fs)
        s_sin = OP("scalar", lambda s: s.activation(csn_s, rpl, AF.Sin, scale=TWOPI), inc=CS)
        OP("vector", lambda v: v.tensor_scalar(mpl, mpl, 0.25, None, ALU.add), inc=CV)
        W("vector", CS, s_sin)
        OP("vector", lambda v: v.tensor_scalar(rpl, mpl, MAGIC, None, ALU.add), inc=CV)
        OP("vector", lambda v: v.tensor_scalar(rpl, rpl, -MAGIC, None, ALU.add), inc=CV)
        OP("vector", lambda v: v.tensor_sub(rpl, mpl, rpl), inc=CV)
        v_fc = CV.n
        W("scalar", CV, v_fc)
        s_cos = OP("scalar", lambda s: s.activation(csn_c, rpl, AF.Sin, scale=TWOPI), inc=CS)

        if pc == 0:
            W("tensor", CV, v_2a_done)
            W("tensor", CS, s_2a_done)
        W("vector", CS, s_cos)
        for lc in range(4):
            bre = 2 + lc % 2
            bim = 4 + lc % 2
            t_bu = {}
            for comp in range(2):
                bank = bre if comp == 0 else bim
                W("tensor", CV, bu_free[bank])
                for dc in range(NDC):
                    OP("tensor", lambda t, comp=comp, lc=lc, dc=dc, pc=pc, bank=bank:
                       t.matmul(ps[bank][:, 0:512], bt_pl[comp][dc][:, pc * 128:(pc + 1) * 128],
                                xn_bf[:, dc * 2048 + lc * 512: dc * 2048 + (lc + 1) * 512],
                                start=(dc == 0), stop=(dc == NDC - 1)), inc=CT)
                t_bu[comp] = CT.n
            t_bu_all = t_bu[1]
            sl = slice(lc * 512, (lc + 1) * 512)
            cs_c = csn_c[:, sl]
            cs_s = csn_s[:, sl]
            W("vector", CT, t_bu[1])
            OP("vector", lambda v, sl=sl, cs_c=cs_c, bre=bre: v.tensor_tensor(btw[0][:, sl], cs_c, ps[bre][:, 0:512], ALU.mult), inc=CV)
            OP("vector", lambda v, sl=sl, cs_s=cs_s, bim=bim: v.tensor_tensor(rpl[:, sl], cs_s, ps[bim][:, 0:512], ALU.mult), inc=CV)
            OP("vector", lambda v, sl=sl: v.tensor_add(btw[0][:, sl], btw[0][:, sl], rpl[:, sl]), inc=CV)
            OP("vector", lambda v, sl=sl, cs_c=cs_c, bim=bim: v.tensor_tensor(btw[1][:, sl], cs_c, ps[bim][:, 0:512], ALU.mult), inc=CV)
            OP("vector", lambda v, sl=sl, cs_s=cs_s, bre=bre: v.tensor_tensor(rpl[:, sl], cs_s, ps[bre][:, 0:512], ALU.mult), inc=CV)
            OP("vector", lambda v, sl=sl: v.tensor_sub(btw[1][:, sl], btw[1][:, sl], rpl[:, sl]), inc=CV)
            bu_free[bre] = CV.n
            bu_free[bim] = CV.n
        rho_b = pvc(pc, RHO).broadcast_to([128, 2048])
        OP("vector", lambda v, rho_b=rho_b: v.tensor_tensor_scan(btw[0][:], rho_b, btw[0][:], 0.0, ALU.mult, ALU.add), inc=CV)
        OP("vector", lambda v, rho_b=rho_b: v.tensor_tensor_scan(btw[1][:], rho_b, btw[1][:], 0.0, ALU.mult, ALU.add), inc=CV)
        OP("vector", lambda v, pc=pc: v.tensor_tensor(xs_pl[0][pc], csn_c, btw[0][:], ALU.mult), inc=CV)
        OP("vector", lambda v: v.tensor_tensor(rpl, csn_s, btw[1][:], ALU.mult), inc=CV)
        OP("vector", lambda v, pc=pc: v.tensor_sub(xs_pl[0][pc], xs_pl[0][pc], rpl), inc=CV)
        OP("vector", lambda v, pc=pc: v.tensor_tensor(xs_pl[1][pc], csn_c, btw[1][:], ALU.mult), inc=CV)
        OP("vector", lambda v: v.tensor_tensor(rpl, csn_s, btw[0][:], ALU.mult), inc=CV)
        OP("vector", lambda v, pc=pc: v.tensor_add(xs_pl[1][pc], xs_pl[1][pc], rpl), inc=CV)
        v_scr_free = CV.n
    v_2b_done = CV.n

    if debug:
        W("sync", CV, v_2b_done)
        dma("sync", dbg_pv, pv, CSTD)
        dma("sync", dbg_xs, bA[:, SCRB + 0: SCRB + 16384], CSTD)
        dma("sync", dbg_bt, bA[:, SCRB + 16384: SCRB + 24576], CSTD)
        dma("sync", dbg_cs, bA[:, SCRB + 24576: SCRB + 28672], CSTD)
    v_dbg2b_gate = CSTD.n

    craw = [fsl(0, 1024), fsl(1024, 1024)]
    csc = fsl(2048, 1024)
    v_craw_done = [v_scr_free, v_scr_free]
    t_csc_done = 0
    v_2c_evac = [0, v_2a_evac[1]]
    s_2c_evac = [s_2a_evac[0], 0]
    for hc in range(NDC):
        par = hc % 2
        q = "sync"
        qc = CU
        W(q, CV, v_craw_done[par])
        if debug and hc == 0:
            W(q, CSTD, v_dbg2b_gate)
        u_c = dma(q, craw[par], c_d[hc * 128:(hc + 1) * 128].rearrange("h p c -> h (p c)"), qc)
        W("vector", qc, u_c)
        W("vector", CT, t_csc_done)
        OP("vector", lambda v, par=par: v.tensor_copy(csc[:, 0:512], craw[par][:, 0::2]), inc=CV)
        OP("vector", lambda v, par=par: v.tensor_scalar(csc[:, 512:1024], craw[par][:, 1::2], -1.0, None, ALU.mult), inc=CV)
        v_craw_done[par] = CV.n
        v_csc = CV.n
        W("tensor", CV, v_csc)
        for pc in range(NPC):
            for comp in range(2):
                bank = (pc * 2 + comp) % 2
                if bank == 1:
                    W("tensor", CV, v_2c_evac[1])
                else:
                    W("tensor", CS, s_2c_evac[0])
                OP("tensor", lambda t, pc=pc, comp=comp, bank=bank:
                   t.transpose(ps[bank][:, 0:128], csc[:, comp * 512 + pc * 128: comp * 512 + (pc + 1) * 128], ident[:]), inc=CT)
                t_ev = CT.n
                if bank == 1:
                    W("vector", CT, t_ev)
                    OP("vector", lambda v, pc=pc, comp=comp, hc=hc, bank=bank:
                       v.tensor_copy(ct_pl[pc][comp][:, hc * 128:(hc + 1) * 128], ps[bank][:, 0:128]), inc=CV)
                    v_2c_evac[1] = CV.n
                else:
                    W("scalar", CT, t_ev)
                    OP("scalar", lambda s, pc=pc, comp=comp, hc=hc, bank=bank:
                       s.copy(ct_pl[pc][comp][:, hc * 128:(hc + 1) * 128], ps[bank][:, 0:128]), inc=CS)
                    s_2c_evac[0] = CS.n
        t_csc_done = CT.n
    v_2c_done = CV.n
    s_2c_done = CS.n

    if debug:
        W("sync", CV, v_2c_done)
        W("sync", CT, t_csc_done)
        dma("sync", dbg_csc, fA[:, SCRF + 0: SCRF + 3072], CSTD)

    ub2 = [fsl(0, 1024), fsl(1024, 1024)]
    t1buf = fsl(2048, 512)
    dp1 = fsl(2560, 1024)
    d_row = fsl(2560, 1024, parts=1)
    b2_sb = fsl(3584, 1024, parts=8)
    W("sync", CV, v_2c_done)
    W("sync", CS, s_2c_done)
    W("sync", CT, t_csc_done)
    if debug:
        W("sync", CSTD, CSTD.n)
    u_d2 = dma("sync", d_row, dv_d.rearrange("(o n) -> o n", o=1), CU)
    u_b2 = dma("sync", b2_sb, b2_d, CU)
    W("tensor", CU, u_d2)
    W("tensor", CV, v_wem[15])
    W("tensor", CV, v_wem[14])
    for hcol in range(2):
        t_ev = OP("tensor", lambda t, hcol=hcol: t.matmul(ps[7][:, 0:512], ones1[0:1, :],
                                                          d_row[0:1, hcol * 512:(hcol + 1) * 512], start=True, stop=True), inc=CT)
        W("vector", CT, t_ev)
        OP("vector", lambda v, hcol=hcol: v.tensor_copy(dp1[:, hcol * 512:(hcol + 1) * 512], ps[7][:, 0:512]), inc=CV)
        if hcol == 0:
            W("tensor", CV, CV.n)
    v_dp1 = CV.n
    vf()

    ys_free = {2: bu_free[2], 3: bu_free[3]}
    v_init_ev = [0] * NTT
    s_init_ev = [0] * NTT
    for tt in range(NTT):
        par = tt % 2
        q = "sync"
        qc = CU
        if tt >= 2:
            W(q, CV, v_init_ev[tt - 2])
        u2_ev = dma(q, ub2[par], u_d[tt * 128:(tt + 1) * 128, :], qc)
        if tt == 0:
            W("tensor", CU, u_b2)
            W("tensor", CV, v_2c_done)
            W("tensor", CS, s_2c_done)
            W("tensor", CV, v_2b_done)
        for hcol in range(2):
            bank = 2 + hcol
            W("tensor", CV, ys_free[bank])
            for pc in range(NPC):
                for comp in range(2):
                    OP("tensor", lambda t, pc=pc, comp=comp, tt=tt, hcol=hcol, bank=bank:
                       t.matmul(ps[bank][:, 0:512], xs_pl[comp][pc][:, tt * 128:(tt + 1) * 128],
                                ct_pl[pc][comp][:, hcol * 512:(hcol + 1) * 512],
                                start=(pc == 0 and comp == 0), stop=False), inc=CT)
            t_ys = OP("tensor", lambda t, tt=tt, hcol=hcol, bank=bank:
                      t.matmul(ps[bank][:, 0:512], w_em[0:8, tt * 128:(tt + 1) * 128],
                               b2_sb[0:8, hcol * 512:(hcol + 1) * 512], start=False, stop=True), inc=CT)
            W("vector", qc, u2_ev)
            W("vector", CT, t_ys)
            OP("vector", lambda v, par=par, hcol=hcol, tt=tt:
               v.tensor_scalar(t1buf, ub2[par][:, hcol * 512:(hcol + 1) * 512],
                               stats[:, 2 * tt:2 * tt + 1], stats[:, 2 * tt + 1:2 * tt + 2],
                               ALU.add, ALU.mult), inc=CV)
            if tt == 0 and hcol == 0:
                W("vector", CV, v_dp1)
            OP("vector", lambda v, hcol=hcol:
               v.tensor_tensor(t1buf, t1buf, dp1[:, hcol * 512:(hcol + 1) * 512], ALU.mult), inc=CV)
            OP("vector", lambda v, par=par, hcol=hcol:
               v.tensor_tensor(t1buf, t1buf, ub2[par][:, hcol * 512:(hcol + 1) * 512], ALU.add), inc=CV)
            OP("vector", lambda v, tt=tt, hcol=hcol, bank=bank:
               v.tensor_tensor(out_acc[:, tt * 1024 + hcol * 512: tt * 1024 + (hcol + 1) * 512],
                               t1buf, ps[bank][:, 0:512], ALU.add), inc=CV)
            ys_free[bank] = CV.n
            v_t1_used = CV.n
        v_init_ev[tt] = CV.n
    t_ys_all = CT.n
    v_2d_done = CV.n

    if debug:
        W("sync", CV, v_phase1)
        dma("sync", dbg_w, w_rt, CSTD)
        dma("sync", dbg_xn, xn_bf, CSTD)
        W("sync", CV, v_2d_done)
        dma("sync", dbg_ct, bA[:, SCRB + 16384: SCRB + 24576], CSTD)
        for tt in range(NTT):
            dma("sync", dbg_acc[tt * 128:(tt + 1) * 128, :], out_acc[:, tt * 1024:(tt + 1) * 1024], CSTD)
        W("sync", CSTD, CSTD.n)

    w1q = [bsl(0, 8192), bsl(8192, 8192)]
    w2q = [bsl(16384, 8192), bsl(24576, 8192)]
    h_bf = [bsl(32768, 4096), bsl(36864, 4096)]
    cw_base = CW.n
    t_phase_end = [0] * 32
    t_h_ev = []
    s_gl_ev = []
    s_tb_rdy = []
    t_o_ev = []
    v_o_ev = []
    t_o_last_of_g = []

    def gemm1(phi, tb):
        e, q = eorder[phi // 4], phi % 4
        par = phi % 2
        g = phi * 4 + tb
        hb = h_bf[g % 2]
        for ht in range(8):
            g_h = len(t_h_ev)
            bank = g_h % 2
            if g_h >= 2:
                W("tensor", CS, s_gl_ev[g_h - 2])
            elif phi == 0:
                W("tensor", CV, v_2d_done)
            for dc in range(NDC):
                OP("tensor", lambda t, dc=dc, ht=ht, tb=tb, par=par, bank=bank:
                   t.matmul(ps[bank][:, 0:512], w1q[par][:, dc * 1024 + ht * 128: dc * 1024 + (ht + 1) * 128],
                            xn_bf[:, dc * 2048 + tb * 512: dc * 2048 + (tb + 1) * 512],
                            start=(dc == 0), stop=(dc == NDC - 1)), inc=CT)
            t_h_ev.append(CT.n)
            W("scalar", CT, t_h_ev[-1])
            if g >= 2:
                W("scalar", CT, t_o_last_of_g[g - 2])
            s_gl_ev.append(OP("scalar", lambda s, ht=ht, e=e, q=q, bank=bank, hb=hb:
                              s.activation(hb[:, ht * 512:(ht + 1) * 512], ps[bank][:, 0:512], AF.Gelu,
                                           bias=b1c[:, e * 32 + q * 8 + ht: e * 32 + q * 8 + ht + 1], scale=1.0), inc=CS))
        OP("scalar", lambda s, hb=hb: s.activation(hb[:, 3584:4096], hb[:, 3584:4096], AF.Copy), inc=CS)
        s_tb_rdy.append(CS.n)

    def gemm2(phi, tb):
        e, q = eorder[phi // 4], phi % 4
        par = phi % 2
        g = phi * 4 + tb
        hb = h_bf[g % 2]
        W("tensor", CS, s_tb_rdy[g])
        for ts_ in range(4):
            for dc2 in range(2):
                j = len(t_o_ev)
                bank = 2 + j % 4
                if j >= 4:
                    W("tensor", CV, v_o_ev[j - 4])
                elif j == 0:
                    W("tensor", CV, v_2d_done)
                if debug and j == 0:
                    W("vector", CSTD, CSTD.n)
                for hc in range(8):
                    OP("tensor", lambda t, hc=hc, ts_=ts_, dc2=dc2, par=par, bank=bank, hb=hb:
                       t.matmul(ps[bank][:, 0:512], hb[:, hc * 512 + ts_ * 128: hc * 512 + (ts_ + 1) * 128],
                                w2q[par][:, hc * 1024 + dc2 * 512: hc * 1024 + (dc2 + 1) * 512],
                                start=(hc == 0), stop=(hc == 7)), inc=CT)
                t_o_ev.append(CT.n)
                tt = tb * 4 + ts_
                W("vector", CT, t_o_ev[-1])
                v_o_ev.append(OP("vector", lambda v, tt=tt, dc2=dc2, e=e, bank=bank:
                                 v.scalar_tensor_tensor(out_acc[:, tt * 1024 + dc2 * 512: tt * 1024 + (dc2 + 1) * 512],
                                                        ps[bank][:, 0:512], w_rt[:, tt * 8 + e: tt * 8 + e + 1],
                                                        out_acc[:, tt * 1024 + dc2 * 512: tt * 1024 + (dc2 + 1) * 512],
                                                        ALU.mult, ALU.add), inc=CV))
        t_o_last_of_g.append(t_o_ev[-1])

    for phi in range(32):
        e, q = eorder[phi // 4], phi % 4
        edr = dram_perm[e]
        par = phi % 2
        if phi < 2:
            W("gpsimd", CT, t_ys_all)
            W("gpsimd", CV, v_2b_done)
            if debug:
                W("gpsimd", CSTD, CSTD.n)
        else:
            W("gpsimd", CT, t_phase_end[phi - 2])
        dma("gpsimd", w1q[par].rearrange("p (dc h) -> p dc h", dc=8),
            w1_d[edr][:, q * 1024:(q + 1) * 1024].rearrange("(dc p) h -> p dc h", p=128), CW)
        dma("gpsimd", w2q[par].rearrange("p (hc d) -> p hc d", hc=8),
            w2_d[edr][q * 1024:(q + 1) * 1024, :].rearrange("(hc p) d -> p hc d", p=128), CW)
        W("tensor", CW, cw_base + 32 * (phi + 1))
        gemm1(phi, 0)
        gemm1(phi, 1)
        gemm2(phi, 0)
        gemm1(phi, 2)
        gemm2(phi, 1)
        gemm1(phi, 3)
        gemm2(phi, 2)
        gemm2(phi, 3)
        t_phase_end[phi] = CT.n

    OP("vector", lambda v: v.tensor_copy(t1buf[:, 0:512], out_acc[:, 0:512]), inc=CV)
    OP("vector", lambda v: v.tensor_copy(t1buf[:, 0:512], out_acc[:, 512:1024]), inc=CV)
    OP("vector", lambda v: v.tensor_copy(t1buf[:, 0:512], out_acc[:, 1024:1536]), inc=CV)
    v_tail = CV.n
    for tt in range(NTT):
        j_last = ((31 * 4 + tt // 4) * 4 + (tt % 4)) * 2 + 1
        W("sync", CV, min(v_o_ev[j_last] + 3, v_tail))
        dma("sync", out_d[tt * 128:(tt + 1) * 128, :], out_acc[:, tt * 1024:(tt + 1) * 1024], CST)
    W("sync", CST, CST.n)

    with nc.Block() as block:
        @block.tensor
        def _(t):
            for fn in plan.streams["tensor"]:
                fn(t)

        @block.vector
        def _(v):
            for fn in plan.streams["vector"]:
                fn(v)

        @block.scalar
        def _(s):
            for fn in plan.streams["scalar"]:
                fn(s)

        @block.sync
        def _(s):
            for fn in plan.streams["sync"]:
                fn(s)

        @block.gpsimd
        def _(g):
            for fn in plan.streams["gpsimd"]:
                fn(g)

    return nc


_NC_CACHE = None


def kernel(L0_attn_u, L0_attn_Lambda_re, L0_attn_Lambda_im, L0_attn_B, L0_attn_C,
           L0_attn_D, L0_attn_log_step, L0_ffn_W1, L0_ffn_b1, L0_ffn_W2, L0_ffn_b2,
           L0_ffn_Wr, L0_ffn_br):
    global _NC_CACHE
    if _NC_CACHE is None:
        _NC_CACHE = build_program()
    nc = _NC_CACHE

    u = np.ascontiguousarray(np.asarray(L0_attn_u, dtype=np.float32))
    com = {
        "lre": np.ascontiguousarray(np.asarray(L0_attn_Lambda_re, dtype=np.float32)),
        "lim": np.ascontiguousarray(np.asarray(L0_attn_Lambda_im, dtype=np.float32)),
        "lstep": np.ascontiguousarray(np.asarray(L0_attn_log_step, dtype=np.float32)),
        "bmat": np.ascontiguousarray(np.asarray(L0_attn_B, dtype=np.float32)),
        "cmat": np.ascontiguousarray(np.asarray(L0_attn_C, dtype=np.float32)),
        "dvec": np.ascontiguousarray(np.asarray(L0_attn_D, dtype=np.float32)),
        "w1": np.ascontiguousarray(np.asarray(L0_ffn_W1, dtype=np.float32)),
        "b1": np.ascontiguousarray(np.asarray(L0_ffn_b1, dtype=np.float32)),
        "w2": np.ascontiguousarray(np.asarray(L0_ffn_W2, dtype=np.float32)),
        "b2": np.ascontiguousarray(np.asarray(L0_ffn_b2, dtype=np.float32)),
        "wr": np.ascontiguousarray(np.asarray(L0_ffn_Wr, dtype=np.float32)),
        "br": np.ascontiguousarray(np.asarray(L0_ffn_br, dtype=np.float32)),
    }
    in_maps = []
    for c in range(8):
        m = dict(com)
        m["u"] = np.ascontiguousarray(u[c])
        in_maps.append(m)
    res = run_bass_kernel_spmd(nc, in_maps, core_ids=list(range(8)))
    out = np.stack([np.asarray(res.results[c]["out"]) for c in range(8)], axis=0)
    return out.astype(np.float32)


# version 07
# speedup vs baseline: 1.0646x; 1.0646x over previous
import numpy as np
import concourse.bass as bass
import concourse.mybir as mybir
from concourse.bass_utils import run_bass_kernel_spmd

F32 = mybir.dt.float32
BF = mybir.dt.bfloat16
AF = mybir.ActivationFunctionType
ALU = mybir.AluOpType
AXX = mybir.AxisListType.X
MAGIC = 12582912.0
INV2PI = float(1.0 / (2.0 * np.pi))
TWOPI = float(2.0 * np.pi)

T_TOK, DM, P, E, H = 2048, 1024, 512, 8, 4096
NTT, NDC, NPC = 16, 8, 4


class Ctr:
    def __init__(self, name):
        self.name, self.n, self.h = name, 0, None


class Plan:
    def __init__(self):
        self.streams = {"tensor": [], "vector": [], "scalar": [], "sync": [], "gpsimd": []}

    def emit(self, eng, fn):
        self.streams[eng].append(fn)

    def wait(self, eng, ctr, val):
        if val <= 0:
            return
        assert val <= ctr.n, f"{eng} waits {ctr.name}>={val}, only {ctr.n} planned"
        self.emit(eng, lambda e, c=ctr, v=val: e.wait_ge(c.h, v))

    def op(self, eng, fn, inc=None, k=1):
        if inc is None:
            self.emit(eng, fn)
            return None
        inc.n += k
        self.emit(eng, lambda e, fn=fn, c=inc, k=k: fn(e).then_inc(c.h, k))
        return inc.n


def build_program(debug=False, eorder=None, dram_perm=None, p1q=True):
    if eorder is None:
        eorder = list(range(8))
    if dram_perm is None:
        dram_perm = list(range(8))
    nc = bass.Bass("TRN2", target_bir_lowering=False)

    u_d = nc.dram_tensor("u", [T_TOK, DM], F32, kind="ExternalInput").ap()
    lre_d = nc.dram_tensor("lre", [P], F32, kind="ExternalInput").ap()
    lim_d = nc.dram_tensor("lim", [P], F32, kind="ExternalInput").ap()
    lstep_d = nc.dram_tensor("lstep", [P, 1], F32, kind="ExternalInput").ap()
    b_d = nc.dram_tensor("bmat", [P, DM, 2], F32, kind="ExternalInput").ap()
    c_d = nc.dram_tensor("cmat", [DM, P, 2], F32, kind="ExternalInput").ap()
    dv_d = nc.dram_tensor("dvec", [DM], F32, kind="ExternalInput").ap()
    w1_d = nc.dram_tensor("w1", [E, DM, H], F32, kind="ExternalInput").ap()
    b1_d = nc.dram_tensor("b1", [E, H], F32, kind="ExternalInput").ap()
    w2_d = nc.dram_tensor("w2", [E, H, DM], F32, kind="ExternalInput").ap()
    b2_d = nc.dram_tensor("b2", [E, DM], F32, kind="ExternalInput").ap()
    wr_d = nc.dram_tensor("wr", [DM, E], F32, kind="ExternalInput").ap()
    br_d = nc.dram_tensor("br", [E], F32, kind="ExternalInput").ap()
    out_d = nc.dram_tensor("out", [T_TOK, DM], F32, kind="ExternalOutput").ap()
    if debug:
        dbg_w = nc.dram_tensor("dbg_w", [128, 128], F32, kind="ExternalOutput").ap()
        dbg_acc = nc.dram_tensor("dbg_acc", [T_TOK, DM], F32, kind="ExternalOutput").ap()
        dbg_xn = nc.dram_tensor("dbg_xn", [128, 16384], BF, kind="ExternalOutput").ap()
        dbg_xs = nc.dram_tensor("dbg_xs", [128, 16384], BF, kind="ExternalOutput").ap()
        dbg_bt = nc.dram_tensor("dbg_bt", [128, 8192], BF, kind="ExternalOutput").ap()
        dbg_cs = nc.dram_tensor("dbg_cs", [128, 4096], BF, kind="ExternalOutput").ap()
        dbg_pv = nc.dram_tensor("dbg_pv", [128, 64], F32, kind="ExternalOutput").ap()
        dbg_pv0 = nc.dram_tensor("dbg_pv0", [128, 64], F32, kind="ExternalOutput").ap()
        dbg_ct = nc.dram_tensor("dbg_ct", [128, 8192], BF, kind="ExternalOutput").ap()
        dbg_csc = nc.dram_tensor("dbg_csc", [128, 3072], F32, kind="ExternalOutput").ap()
        dbg_id = nc.dram_tensor("dbg_id", [128, 128], F32, kind="ExternalOutput").ap()

    plan = Plan()
    CT, CV, CS, CG = Ctr("T"), Ctr("V"), Ctr("S"), Ctr("G")
    CU, CW, CF, CST = Ctr("U"), Ctr("W"), Ctr("F"), Ctr("ST")
    ctrs = [CT, CV, CS, CG, CU, CW, CF, CST]

    NFS = 4608
    NF = 16384 + 128 + 256 + 64 + 128 + 128 + 8 + 1 + 64 + 64 + 32 + 2048 + NFS
    NB = 16384 + 36864 + 4096
    fA = nc.sbuf_tensor("fA", [128, NF], F32).__enter__()
    bA = nc.sbuf_tensor("bA", [128, NB], BF).__enter__()

    off = [0]
    def takeF(cols, parts=128):
        ap = fA[0:parts, off[0]:off[0] + cols]
        off[0] += cols
        assert off[0] <= NF
        return ap

    out_acc = takeF(16384)
    w_rt = takeF(128)
    b1c = takeF(256)
    wr_sb = takeF(64)
    ident = takeF(128)
    ones1 = takeF(128, parts=1)
    br_row = takeF(8, parts=1)
    epsc = takeF(1)
    pv = takeF(64)
    sm = takeF(64)
    stats = takeF(32)
    w_em = takeF(2048, parts=8)
    SCRF = off[0]
    takeF(NFS)

    def fsl(o, cols, parts=128):
        assert o + cols <= NFS
        return fA[0:parts, SCRF + o: SCRF + o + cols]

    xn_bf = bA[:, 0:16384]
    SCRB = 16384
    def bsl(o, cols):
        assert o + cols <= 36864 + 4096
        return bA[:, SCRB + o: SCRB + o + cols]

    RHO, THP, SGR, SGI, SGIN, T0, T1, T2, T3, T4, T5, T6 = range(12)
    def pva(q):
        return pv[:, q::16]
    def pvc(pc, q):
        return pv[:, pc * 16 + q: pc * 16 + q + 1]

    ps = []
    for i in range(8):
        ps.append(nc.psum_tensor(f"ps{i}", [128, 512], F32).__enter__())
    for c in ctrs:
        c.h = nc.semaphore(c.name).__enter__()
    CSTD = Ctr("STD")
    CSTD.h = nc.semaphore("STD").__enter__()

    W, OP = plan.wait, plan.op

    def vf():
        OP("vector", lambda v: v.engine_nop(), inc=CF)
        W("vector", CF, CF.n)

    def vpad():
        OP("vector", lambda v: v.tensor_copy(pv[:, 15:16], pv[:, 14:15]), inc=CV)
        return CV.n

    def spad():
        OP("scalar", lambda s: s.activation(pv[:, 31:32], pv[:, 31:32], AF.Copy), inc=CS)
        return CS.n

    def dma(eng, out_ap, in_ap, ctr):
        return OP(eng, lambda e, o=out_ap, i=in_ap: e.dma_start(out=o, in_=i), inc=ctr, k=16)

    lrel4 = fsl(0, 128, parts=4)
    liml4 = fsl(128, 128, parts=4)
    lsl4 = fsl(256, 128, parts=4)
    iotad = fsl(384, 128)
    b1st = [fsl(512, 128, parts=32), fsl(640, 128, parts=32)]

    u_l0 = dma("sync", lrel4, lre_d.rearrange("(c p) -> c p", p=128), CU)
    dma("sync", liml4, lim_d.rearrange("(c p) -> c p", p=128), CU)
    u_l2 = dma("sync", lsl4, lstep_d.rearrange("(c p) o -> c (p o)", p=128), CU)
    u_br = dma("sync", br_row, br_d.rearrange("(o n) -> o n", o=1), CU)
    for c in range(NDC):
        u_wr = dma("sync", wr_sb[:, c * 8:(c + 1) * 8],
                   wr_d.rearrange("(c p) e -> c p e", p=128)[c], CU)

    OP("gpsimd", lambda g: g.iota(iotad, [[1, 128]], channel_multiplier=-1,
                                  allow_small_or_imprecise_dtypes=True), inc=CG)
    rampr = fsl(768, 128, parts=1)
    rowc = fsl(896, 1)
    W("vector", CU, u_br)
    OP("vector", lambda v: v.tensor_scalar(ones1[:], lrel4[0:1, 0:128], 0.0, 1.0, ALU.mult, ALU.add), inc=CV)
    vf()
    OP("vector", lambda v: v.tensor_tensor_scan(rampr, ones1[:], ones1[:], 0.0, ALU.mult, ALU.add), inc=CV)
    vf()
    v_ramp = vpad()
    W("sync", CV, v_ramp)
    u_rowc = dma("sync", rowc, rampr, CU)
    W("tensor", CV, v_ramp)
    t_bc = OP("tensor", lambda t: t.matmul(ps[6][:, 384:512], ones1[:], rampr, start=True, stop=True), inc=CT)
    W("vector", CT, t_bc)
    W("vector", CU, u_rowc)
    OP("vector", lambda v: v.tensor_scalar(ident[:], ps[6][:, 384:512], rowc[:, 0:1], None, ALU.is_equal), inc=CV)
    OP("vector", lambda v: v.tensor_scalar(epsc[:], rowc[:, 0:1], 0.0, 1e-5, ALU.mult, ALU.add), inc=CV)
    vf()
    v_const = vpad()

    W("tensor", CU, u_br)
    W("tensor", CV, v_const)
    t3 = []
    for i, src in enumerate([lrel4, liml4, lsl4]):
        t3.append(OP("tensor", lambda t, s=src, i=i:
                     t.transpose(ps[6][:, i * 4:(i + 1) * 4], s, ident[0:4, 0:4]), inc=CT))
    W("vector", CT, t3[-1])
    for i, q in enumerate([T0, T1, T2]):
        OP("vector", lambda v, i=i, q=q: v.tensor_copy(pva(q), ps[6][:, i * 4:(i + 1) * 4]), inc=CV)
    vf()
    v_pv = vpad()
    W("scalar", CV, v_pv)
    s_ev = OP("scalar", lambda s: s.activation(pva(T2), pva(T2), AF.Exp), inc=CS)
    spad()
    W("vector", CS, s_ev + 1)
    OP("vector", lambda v: v.tensor_scalar(pva(T0), pva(T0), -1e-4, None, ALU.min), inc=CV); vf()
    OP("vector", lambda v: v.tensor_mul(pva(T3), pva(T0), pva(T2)), inc=CV); vf()
    OP("vector", lambda v: v.tensor_mul(pva(T4), pva(T1), pva(T2)), inc=CV); vf()
    OP("vector", lambda v: v.tensor_scalar(pva(THP), pva(T4), INV2PI, None, ALU.mult), inc=CV); vf()
    v_thp = vpad()
    W("scalar", CV, v_thp)
    OP("scalar", lambda s: s.activation(pva(RHO), pva(T3), AF.Exp), inc=CS)

    def frac4(src_q, dst_q, tmp_q, plus_quarter):
        base = src_q
        if plus_quarter:
            OP("vector", lambda v: v.tensor_scalar(pva(tmp_q), pva(src_q), 0.25, None, ALU.add), inc=CV); vf()
            base = tmp_q
        OP("vector", lambda v, b=base, d=dst_q: v.tensor_scalar(pva(d), pva(b), MAGIC, None, ALU.add), inc=CV); vf()
        OP("vector", lambda v, d=dst_q: v.tensor_scalar(pva(d), pva(d), -MAGIC, None, ALU.add), inc=CV); vf()
        OP("vector", lambda v, b=base, d=dst_q: v.tensor_sub(pva(d), pva(b), pva(d)), inc=CV); vf()

    frac4(THP, T5, T6, False)
    frac4(THP, T2, T6, True)
    v_frac = vpad()
    W("scalar", CV, v_frac)
    OP("scalar", lambda s: s.activation(pva(T5), pva(T5), AF.Sin, scale=TWOPI), inc=CS)
    OP("scalar", lambda s: s.activation(pva(T2), pva(T2), AF.Sin, scale=TWOPI), inc=CS)
    s_trig = spad()
    W("vector", CS, s_trig)
    OP("vector", lambda v: v.tensor_mul(pva(T5), pva(RHO), pva(T5)), inc=CV); vf()
    OP("vector", lambda v: v.tensor_mul(pva(T2), pva(RHO), pva(T2)), inc=CV); vf()
    OP("vector", lambda v: v.tensor_scalar(pva(T2), pva(T2), -1.0, None, ALU.add), inc=CV); vf()
    OP("vector", lambda v: v.tensor_mul(pva(T6), pva(T0), pva(T0)), inc=CV); vf()
    OP("vector", lambda v: v.tensor_mul(pva(T3), pva(T1), pva(T1)), inc=CV); vf()
    OP("vector", lambda v: v.tensor_add(pva(T6), pva(T6), pva(T3)), inc=CV); vf()
    OP("vector", lambda v: v.reciprocal(pva(T6), pva(T6)), inc=CV); vf()
    OP("vector", lambda v: v.tensor_mul(pva(T3), pva(T2), pva(T0)), inc=CV); vf()
    OP("vector", lambda v: v.tensor_mul(pva(T4), pva(T5), pva(T1)), inc=CV); vf()
    OP("vector", lambda v: v.tensor_add(pva(T3), pva(T3), pva(T4)), inc=CV); vf()
    OP("vector", lambda v: v.tensor_mul(pva(SGR), pva(T3), pva(T6)), inc=CV); vf()
    OP("vector", lambda v: v.tensor_mul(pva(T3), pva(T5), pva(T0)), inc=CV); vf()
    OP("vector", lambda v: v.tensor_mul(pva(T4), pva(T2), pva(T1)), inc=CV); vf()
    OP("vector", lambda v: v.tensor_sub(pva(T3), pva(T3), pva(T4)), inc=CV); vf()
    OP("vector", lambda v: v.tensor_mul(pva(SGI), pva(T3), pva(T6)), inc=CV); vf()
    OP("vector", lambda v: v.tensor_scalar(pva(SGIN), pva(SGI), -1.0, None, ALU.mult), inc=CV)
    v_sigma = CV.n
    vf()

    if debug:
        W("sync", CV, v_sigma)
        dma("sync", dbg_pv0, pv, CSTD)
        dma("sync", dbg_id, ident, CSTD)

    t_b1 = []
    v_b1 = []
    for e in range(E):
        par = e % 2
        if e >= 2:
            W("sync", CV, v_b1[e - 2])
        u_b1 = dma("sync", b1st[par], b1_d[e].rearrange("(r p) -> r p", p=128), CU)
        W("tensor", CU, u_b1)
        if e >= 1:
            W("tensor", CV, v_b1[e - 1])
        t_b1.append(OP("tensor", lambda t, par=par: t.transpose(ps[6][:, 128:160], b1st[par], ident[0:32, 0:32]), inc=CT))
        W("vector", CT, t_b1[-1])
        v_b1.append(OP("vector", lambda v, e=e: v.tensor_copy(b1c[:, e * 32:(e + 1) * 32], ps[6][:, 128:160]), inc=CV))

    ub = [fsl(0, 1024), fsl(1024, 1024)]
    xnf = [fsl(2048, 1024), fsl(3072, 1024)]
    sqd = xnf[0]
    lg_all = fsl(4096, 128)
    eq_all = fsl(4224, 128)
    elg_all = fsl(4352, 128)
    rs16 = fsl(4480, 96)

    v_red = [0] * NTT
    s_sq = [0] * NTT
    for tt in range(NTT):
        par = tt % 2
        if tt == 0:
            W("sync", CT, t_b1[-1])
            W("sync", CV, v_b1[-1])
        q = "sync"
        qc = CU
        if tt >= 2:
            W(q, CV, v_red[tt - 2])
            W(q, CS, s_sq[tt - 2])
        u_ev = dma(q, ub[par], u_d[tt * 128:(tt + 1) * 128, :], qc)
        W("vector", qc, u_ev)
        v_red[tt] = OP("vector", lambda v, par=par, tt=tt: v.reduce_sum(sm[:, tt:tt + 1], ub[par], axis=AXX), inc=CV)
        W("scalar", qc, u_ev)
        s_sq[tt] = OP("scalar", lambda s, par=par, tt=tt: s.activation(sqd, ub[par], AF.Square,
                                                                      accum_out=sm[:, 16 + tt:16 + tt + 1]), inc=CS)
    W("vector", CS, s_sq[-1])
    vf()
    OP("vector", lambda v: v.tensor_scalar(rs16[:, 80:96], sm[:, 0:16], 1.0 / DM, None, ALU.mult), inc=CV)
    vf()
    OP("vector", lambda v: v.tensor_mul(sm[:, 32:48], rs16[:, 80:96], rs16[:, 80:96]), inc=CV)
    vf()
    OP("vector", lambda v: v.scalar_tensor_tensor(sm[:, 48:64], sm[:, 16:32], 1.0 / DM, sm[:, 32:48],
                                                  ALU.mult, ALU.subtract), inc=CV)
    vf()
    OP("vector", lambda v: v.tensor_scalar(stats[:, 0::2], rs16[:, 80:96], -1.0, None, ALU.mult), inc=CV)
    vf()
    v_nm = vpad()
    W("scalar", CV, v_nm)
    OP("scalar", lambda s: s.activation(sm[:, 48:64], sm[:, 48:64], AF.Ln, bias=epsc[:, 0:1], scale=1.0), inc=CS)
    OP("scalar", lambda s: s.activation(stats[:, 1::2], sm[:, 48:64], AF.Exp, scale=-0.5), inc=CS)
    s_rstd = spad()
    W("vector", CS, s_rstd)
    OP("vector", lambda v: v.tensor_mul(sm[:, 0:16], stats[:, 0::2], stats[:, 1::2]), inc=CV)
    v_nmr = CV.n
    vf()

    v_xn_ev = [0] * NTT
    s_xn_ev = [0] * NTT
    v_lg_ev = [0] * NTT
    t_tr_last = [0] * NTT
    bank01_free = [(CV, 0), (CV, 0)]
    v_p1c = [0] * NTT
    for tt in range(NTT):
        par = tt % 2
        q = "sync"
        qc = CU
        if tt < 2:
            W(q, CV, v_red[NTT - 2 + tt])
            W(q, CS, s_sq[NTT - 2 + tt])
        else:
            W(q, CV, v_p1c[tt - 2])
            W(q, CT, t_tr_last[tt - 2])
        u_ev = dma(q, ub[par], u_d[tt * 128:(tt + 1) * 128, :], qc)
        W("vector", qc, u_ev)
        if tt == 0:
            W("vector", CS, s_rstd)
        OP("vector", lambda v, par=par, tt=tt: v.tensor_scalar(ub[par], ub[par],
                                                               stats[:, 2 * tt:2 * tt + 1],
                                                               stats[:, 2 * tt + 1:2 * tt + 2],
                                                               ALU.add, ALU.mult), inc=CV)
        v_xn_tm = CV.n
        v_p1c[tt] = v_xn_tm

        W("tensor", CV, v_xn_tm)
        for dc in range(NDC):
            bank = dc % 2
            gctr, gval = bank01_free[bank]
            W("tensor", gctr, gval)
            OP("tensor", lambda t, dc=dc, par=par, bank=bank:
               t.transpose(ps[bank][:, 0:128], ub[par][:, dc * 128:(dc + 1) * 128], ident[:]), inc=CT)
            t_ev = CT.n
            if bank == 1:
                W("vector", CT, t_ev)
                OP("vector", lambda v, dc=dc, par=par, bank=bank:
                   v.tensor_copy(xnf[par][:, dc * 128:(dc + 1) * 128], ps[bank][:, 0:128]), inc=CV)
                OP("vector", lambda v, dc=dc, tt=tt, bank=bank:
                   v.tensor_copy(xn_bf[:, dc * 2048 + tt * 128: dc * 2048 + (tt + 1) * 128], ps[bank][:, 0:128]), inc=CV)
                bank01_free[bank] = (CV, CV.n)
            else:
                W("scalar", CT, t_ev)
                OP("scalar", lambda s, dc=dc, par=par, bank=bank:
                   s.copy(xnf[par][:, dc * 128:(dc + 1) * 128], ps[bank][:, 0:128]), inc=CS)
                OP("scalar", lambda s, dc=dc, tt=tt, bank=bank:
                   s.copy(xn_bf[:, dc * 2048 + tt * 128: dc * 2048 + (tt + 1) * 128], ps[bank][:, 0:128]), inc=CS)
                bank01_free[bank] = (CS, CS.n)
        t_tr_last[tt] = CT.n
        v_xn_ev[tt] = CV.n
        s_xn_ev[tt] = CS.n

        W("tensor", CV, v_xn_ev[tt])
        W("tensor", CS, s_xn_ev[tt])
        if tt > 0:
            W("tensor", CV, v_lg_ev[tt - 1])
        W("tensor", CU, u_br)
        for dc in range(NDC):
            OP("tensor", lambda t, dc=dc, par=par:
               t.matmul(ps[2][:, 0:8], xnf[par][:, dc * 128:(dc + 1) * 128],
                        wr_sb[:, dc * 8:(dc + 1) * 8], start=(dc == 0), stop=False), inc=CT)
        t_rt = OP("tensor", lambda t: t.matmul(ps[2][:, 0:8], ones1[:], br_row[:],
                                               start=False, stop=True), inc=CT)
        W("vector", CT, t_rt)
        v_lg_ev[tt] = OP("vector", lambda v, tt=tt: v.tensor_copy(lg_all[:, tt * 8:(tt + 1) * 8], ps[2][:, 0:8]), inc=CV)

    vf()
    for tt in range(NTT):
        OP("vector", lambda v, tt=tt: v.reduce_max(rs16[:, tt:tt + 1], lg_all[:, tt * 8:(tt + 1) * 8], axis=AXX), inc=CV)
    vf()
    OP("vector", lambda v: v.tensor_scalar(rs16[:, 32:48], rs16[:, 0:16], -1.0, None, ALU.mult), inc=CV)
    for tt in range(NTT):
        OP("vector", lambda v, tt=tt: v.tensor_scalar(eq_all[:, tt * 8:(tt + 1) * 8], lg_all[:, tt * 8:(tt + 1) * 8],
                                                      rs16[:, tt:tt + 1], None, ALU.is_equal), inc=CV)
    vf()
    for tt in range(NTT):
        OP("vector", lambda v, tt=tt: v.scalar_tensor_tensor(eq_all[:, tt * 8:(tt + 1) * 8], eq_all[:, tt * 8:(tt + 1) * 8],
                                                             -1e30, lg_all[:, tt * 8:(tt + 1) * 8], ALU.mult, ALU.add), inc=CV)
    vf()
    for tt in range(NTT):
        OP("vector", lambda v, tt=tt: v.reduce_max(rs16[:, 16 + tt:16 + tt + 1], eq_all[:, tt * 8:(tt + 1) * 8], axis=AXX), inc=CV)
    v_m2a = CV.n
    vf()
    W("scalar", CV, v_m2a)
    for tt in range(NTT):
        OP("scalar", lambda s, tt=tt: s.activation(elg_all[:, tt * 8:(tt + 1) * 8], lg_all[:, tt * 8:(tt + 1) * 8],
                                                   AF.Exp, bias=rs16[:, 32 + tt:32 + tt + 1], scale=1.0), inc=CS)
    s_elg = spad()
    W("vector", CS, s_elg)
    for tt in range(NTT):
        OP("vector", lambda v, tt=tt: v.tensor_scalar(eq_all[:, tt * 8:(tt + 1) * 8], lg_all[:, tt * 8:(tt + 1) * 8],
                                                      rs16[:, 16 + tt:16 + tt + 1], None, ALU.is_ge), inc=CV)
    vf()
    OP("vector", lambda v: v.tensor_mul(eq_all[:], elg_all[:], eq_all[:]), inc=CV)
    vf()
    for tt in range(NTT):
        OP("vector", lambda v, tt=tt: v.reduce_sum(rs16[:, 48 + tt:48 + tt + 1], eq_all[:, tt * 8:(tt + 1) * 8], axis=AXX), inc=CV)
    vf()
    OP("vector", lambda v: v.reciprocal(rs16[:, 64:80], rs16[:, 48:64]), inc=CV)
    vf()
    for tt in range(NTT):
        OP("vector", lambda v, tt=tt: v.tensor_scalar(w_rt[:, tt * 8:(tt + 1) * 8], eq_all[:, tt * 8:(tt + 1) * 8],
                                                      rs16[:, 64 + tt:64 + tt + 1], None, ALU.mult), inc=CV)
    v_w_all = CV.n
    vf()
    W("tensor", CV, v_w_all)
    wem_free = [(CV, v_b1[-1]), (CV, 0)]
    v_wem = [0] * NTT
    for tt in range(NTT):
        bank = 6 + tt % 2
        gctr, gval = wem_free[tt % 2]
        W("tensor", gctr, gval)
        t_w = OP("tensor", lambda t, tt=tt, bank=bank: t.transpose(ps[bank][0:8, 256:384], w_rt[:, tt * 8:(tt + 1) * 8], ident[:]), inc=CT)
        W("vector", CT, t_w)
        v_wem[tt] = OP("vector", lambda v, tt=tt, bank=bank: v.tensor_copy(w_em[0:8, tt * 128:(tt + 1) * 128], ps[bank][0:8, 256:384]), inc=CV)
        wem_free[tt % 2] = (CV, v_wem[tt])

    v_rt_ev = [0] * NTT
    v_rt_ev[NTT - 1] = v_lg_ev[NTT - 1]
    v_phase1 = CV.n
    t_phase1 = CT.n
    s_phase1 = CS.n

    xs_pl = [[bsl((comp * 4 + pc) * 2048, 2048) for pc in range(NPC)] for comp in range(2)]
    bt_pl = [[bsl(16384 + (comp * 8 + dc) * 512, 512) for dc in range(NDC)] for comp in range(2)]
    ct_pl = [[bsl(16384 + (pc * 2 + comp) * 1024, 1024) for comp in range(2)] for pc in range(NPC)]
    csn_c = bsl(24576, 2048)
    csn_s = bsl(26624, 2048)
    btw = [bsl(28672, 2048), bsl(30720, 2048)]

    braw = [fsl(0, 1024), fsl(1024, 1024)]
    bsc = fsl(2048, 1024)
    v_braw_done = [0, 0]
    t_bsc_done = 0
    v_2a_evac = [0, 0]
    s_2a_evac = [0, 0]
    W("tensor", CS, s_phase1)
    half = 0
    for pc in range(NPC):
        for dh in range(2):
            par = half % 2
            q = "sync"
            qc = CU
            if half < 2:
                W(q, CV, v_phase1)
                W(q, CT, t_phase1)
                W(q, CS, s_phase1)
            else:
                W(q, CV, v_braw_done[par])
            u_b = dma(q, braw[par],
                      b_d[pc * 128:(pc + 1) * 128, dh * 512:(dh + 1) * 512, :].rearrange("p h c -> p (h c)"), qc)
            W("vector", qc, u_b)
            if half == 0:
                W("vector", CV, v_sigma)
            W("vector", CT, t_bsc_done)
            OP("vector", lambda v, par=par, pc=pc: v.tensor_scalar(bsc[:, 0:512], braw[par][:, 0::2], pvc(pc, SGR), None, ALU.mult), inc=CV)
            OP("vector", lambda v, par=par, pc=pc: v.scalar_tensor_tensor(bsc[:, 0:512], braw[par][:, 1::2], pvc(pc, SGIN), bsc[:, 0:512], ALU.mult, ALU.add), inc=CV)
            OP("vector", lambda v, par=par, pc=pc: v.tensor_scalar(bsc[:, 512:1024], braw[par][:, 1::2], pvc(pc, SGR), None, ALU.mult), inc=CV)
            OP("vector", lambda v, par=par, pc=pc: v.scalar_tensor_tensor(bsc[:, 512:1024], braw[par][:, 0::2], pvc(pc, SGI), bsc[:, 512:1024], ALU.mult, ALU.add), inc=CV)
            v_braw_done[par] = CV.n
            v_bsc = CV.n
            W("tensor", CV, v_bsc)
            for comp in range(2):
                for hcq in range(4):
                    hc = dh * 4 + hcq
                    bank = (comp * 4 + hcq) % 2
                    if bank == 1:
                        gc, gv = CV, v_2a_evac[1]
                    else:
                        gc, gv = CS, s_2a_evac[0]
                    W("tensor", gc, gv)
                    OP("tensor", lambda t, comp=comp, hcq=hcq, bank=bank:
                       t.transpose(ps[bank][:, 0:128], bsc[:, comp * 512 + hcq * 128: comp * 512 + (hcq + 1) * 128], ident[:]), inc=CT)
                    t_ev = CT.n
                    if bank == 1:
                        W("vector", CT, t_ev)
                        OP("vector", lambda v, comp=comp, hc=hc, pc=pc, bank=bank:
                           v.tensor_copy(bt_pl[comp][hc][:, pc * 128:(pc + 1) * 128], ps[bank][:, 0:128]), inc=CV)
                        v_2a_evac[1] = CV.n
                    else:
                        W("scalar", CT, t_ev)
                        OP("scalar", lambda s, comp=comp, hc=hc, pc=pc, bank=bank:
                           s.copy(bt_pl[comp][hc][:, pc * 128:(pc + 1) * 128], ps[bank][:, 0:128]), inc=CS)
                        s_2a_evac[0] = CS.n
            t_bsc_done = CT.n
            half += 1
    v_2a_done = CV.n
    s_2a_done = CS.n

    mpl = fsl(0, 2048)
    rpl = fsl(2048, 2048)
    bu_free = {2: v_lg_ev[NTT - 1], 3: 0, 4: 0, 5: 0}
    v_scr_free = v_2a_done
    t_bu_all = 0
    for pc in range(NPC):
        if pc == 0:
            W("vector", CS, s_2a_done)
            W("vector", CT, t_bsc_done)
        OP("vector", lambda v: v.tensor_scalar(rpl, bt_pl[0][0].broadcast_to([128, 2048]) if False else bA[:, SCRB + 16384: SCRB + 18432], 0.0, 1.0, ALU.mult, ALU.add), inc=CV)
        vf()
        OP("vector", lambda v: v.tensor_tensor_scan(mpl, rpl, rpl, 0.0, ALU.mult, ALU.add), inc=CV)
        vf()
        OP("vector", lambda v, pc=pc: v.tensor_scalar(mpl, mpl, -1.0, pvc(pc, THP), ALU.add, ALU.mult), inc=CV)
        OP("vector", lambda v: v.tensor_scalar(rpl, mpl, MAGIC, None, ALU.add), inc=CV)
        OP("vector", lambda v: v.tensor_scalar(rpl, rpl, -MAGIC, None, ALU.add), inc=CV)
        OP("vector", lambda v: v.tensor_sub(rpl, mpl, rpl), inc=CV)
        v_fs = CV.n
        W("scalar", CV, v_fs)
        s_sin = OP("scalar", lambda s: s.activation(csn_s, rpl, AF.Sin, scale=TWOPI), inc=CS)
        OP("vector", lambda v: v.tensor_scalar(mpl, mpl, 0.25, None, ALU.add), inc=CV)
        W("vector", CS, s_sin)
        OP("vector", lambda v: v.tensor_scalar(rpl, mpl, MAGIC, None, ALU.add), inc=CV)
        OP("vector", lambda v: v.tensor_scalar(rpl, rpl, -MAGIC, None, ALU.add), inc=CV)
        OP("vector", lambda v: v.tensor_sub(rpl, mpl, rpl), inc=CV)
        v_fc = CV.n
        W("scalar", CV, v_fc)
        s_cos = OP("scalar", lambda s: s.activation(csn_c, rpl, AF.Sin, scale=TWOPI), inc=CS)

        if pc == 0:
            W("tensor", CV, v_2a_done)
            W("tensor", CS, s_2a_done)
        W("vector", CS, s_cos)
        for lc in range(4):
            bre = 2 + lc % 2
            bim = 4 + lc % 2
            t_bu = {}
            for comp in range(2):
                bank = bre if comp == 0 else bim
                W("tensor", CV, bu_free[bank])
                for dc in range(NDC):
                    OP("tensor", lambda t, comp=comp, lc=lc, dc=dc, pc=pc, bank=bank:
                       t.matmul(ps[bank][:, 0:512], bt_pl[comp][dc][:, pc * 128:(pc + 1) * 128],
                                xn_bf[:, dc * 2048 + lc * 512: dc * 2048 + (lc + 1) * 512],
                                start=(dc == 0), stop=(dc == NDC - 1)), inc=CT)
                t_bu[comp] = CT.n
            t_bu_all = t_bu[1]
            sl = slice(lc * 512, (lc + 1) * 512)
            cs_c = csn_c[:, sl]
            cs_s = csn_s[:, sl]
            W("vector", CT, t_bu[1])
            OP("vector", lambda v, sl=sl, cs_c=cs_c, bre=bre: v.tensor_tensor(btw[0][:, sl], cs_c, ps[bre][:, 0:512], ALU.mult), inc=CV)
            OP("vector", lambda v, sl=sl, cs_s=cs_s, bim=bim: v.tensor_tensor(rpl[:, sl], cs_s, ps[bim][:, 0:512], ALU.mult), inc=CV)
            OP("vector", lambda v, sl=sl: v.tensor_add(btw[0][:, sl], btw[0][:, sl], rpl[:, sl]), inc=CV)
            OP("vector", lambda v, sl=sl, cs_c=cs_c, bim=bim: v.tensor_tensor(btw[1][:, sl], cs_c, ps[bim][:, 0:512], ALU.mult), inc=CV)
            OP("vector", lambda v, sl=sl, cs_s=cs_s, bre=bre: v.tensor_tensor(rpl[:, sl], cs_s, ps[bre][:, 0:512], ALU.mult), inc=CV)
            OP("vector", lambda v, sl=sl: v.tensor_sub(btw[1][:, sl], btw[1][:, sl], rpl[:, sl]), inc=CV)
            bu_free[bre] = CV.n
            bu_free[bim] = CV.n
        rho_b = pvc(pc, RHO).broadcast_to([128, 2048])
        OP("vector", lambda v, rho_b=rho_b: v.tensor_tensor_scan(btw[0][:], rho_b, btw[0][:], 0.0, ALU.mult, ALU.add), inc=CV)
        OP("vector", lambda v, rho_b=rho_b: v.tensor_tensor_scan(btw[1][:], rho_b, btw[1][:], 0.0, ALU.mult, ALU.add), inc=CV)
        OP("vector", lambda v, pc=pc: v.tensor_tensor(xs_pl[0][pc], csn_c, btw[0][:], ALU.mult), inc=CV)
        OP("vector", lambda v: v.tensor_tensor(rpl, csn_s, btw[1][:], ALU.mult), inc=CV)
        OP("vector", lambda v, pc=pc: v.tensor_sub(xs_pl[0][pc], xs_pl[0][pc], rpl), inc=CV)
        OP("vector", lambda v, pc=pc: v.tensor_tensor(xs_pl[1][pc], csn_c, btw[1][:], ALU.mult), inc=CV)
        OP("vector", lambda v: v.tensor_tensor(rpl, csn_s, btw[0][:], ALU.mult), inc=CV)
        OP("vector", lambda v, pc=pc: v.tensor_add(xs_pl[1][pc], xs_pl[1][pc], rpl), inc=CV)
        v_scr_free = CV.n
    v_2b_done = CV.n

    if debug:
        W("sync", CV, v_2b_done)
        dma("sync", dbg_pv, pv, CSTD)
        dma("sync", dbg_xs, bA[:, SCRB + 0: SCRB + 16384], CSTD)
        dma("sync", dbg_bt, bA[:, SCRB + 16384: SCRB + 24576], CSTD)
        dma("sync", dbg_cs, bA[:, SCRB + 24576: SCRB + 28672], CSTD)
    v_dbg2b_gate = CSTD.n

    craw = [fsl(0, 1024), fsl(1024, 1024)]
    csc = fsl(2048, 1024)
    v_craw_done = [v_scr_free, v_scr_free]
    t_csc_done = 0
    v_2c_evac = [0, v_2a_evac[1]]
    s_2c_evac = [s_2a_evac[0], 0]
    for hc in range(NDC):
        par = hc % 2
        q = "sync"
        qc = CU
        W(q, CV, v_craw_done[par])
        if debug and hc == 0:
            W(q, CSTD, v_dbg2b_gate)
        u_c = dma(q, craw[par], c_d[hc * 128:(hc + 1) * 128].rearrange("h p c -> h (p c)"), qc)
        W("vector", qc, u_c)
        W("vector", CT, t_csc_done)
        OP("vector", lambda v, par=par: v.tensor_copy(csc[:, 0:512], craw[par][:, 0::2]), inc=CV)
        OP("vector", lambda v, par=par: v.tensor_scalar(csc[:, 512:1024], craw[par][:, 1::2], -1.0, None, ALU.mult), inc=CV)
        v_craw_done[par] = CV.n
        v_csc = CV.n
        W("tensor", CV, v_csc)
        for pc in range(NPC):
            for comp in range(2):
                bank = (pc * 2 + comp) % 2
                if bank == 1:
                    W("tensor", CV, v_2c_evac[1])
                else:
                    W("tensor", CS, s_2c_evac[0])
                OP("tensor", lambda t, pc=pc, comp=comp, bank=bank:
                   t.transpose(ps[bank][:, 0:128], csc[:, comp * 512 + pc * 128: comp * 512 + (pc + 1) * 128], ident[:]), inc=CT)
                t_ev = CT.n
                if bank == 1:
                    W("vector", CT, t_ev)
                    OP("vector", lambda v, pc=pc, comp=comp, hc=hc, bank=bank:
                       v.tensor_copy(ct_pl[pc][comp][:, hc * 128:(hc + 1) * 128], ps[bank][:, 0:128]), inc=CV)
                    v_2c_evac[1] = CV.n
                else:
                    W("scalar", CT, t_ev)
                    OP("scalar", lambda s, pc=pc, comp=comp, hc=hc, bank=bank:
                       s.copy(ct_pl[pc][comp][:, hc * 128:(hc + 1) * 128], ps[bank][:, 0:128]), inc=CS)
                    s_2c_evac[0] = CS.n
        t_csc_done = CT.n
    v_2c_done = CV.n
    s_2c_done = CS.n

    if debug:
        W("sync", CV, v_2c_done)
        W("sync", CT, t_csc_done)
        dma("sync", dbg_csc, fA[:, SCRF + 0: SCRF + 3072], CSTD)

    ub2 = [fsl(0, 1024), fsl(1024, 1024)]
    t1buf = fsl(2048, 512)
    dp1 = fsl(2560, 1024)
    d_row = fsl(2560, 1024, parts=1)
    b2_sb = fsl(3584, 1024, parts=8)
    W("sync", CV, v_2c_done)
    W("sync", CS, s_2c_done)
    W("sync", CT, t_csc_done)
    if debug:
        W("sync", CSTD, CSTD.n)
    u_d2 = dma("sync", d_row, dv_d.rearrange("(o n) -> o n", o=1), CU)
    u_b2 = dma("sync", b2_sb, b2_d, CU)
    W("tensor", CU, u_d2)
    W("tensor", CV, v_wem[15])
    W("tensor", CV, v_wem[14])
    for hcol in range(2):
        t_ev = OP("tensor", lambda t, hcol=hcol: t.matmul(ps[7][:, 0:512], ones1[0:1, :],
                                                          d_row[0:1, hcol * 512:(hcol + 1) * 512], start=True, stop=True), inc=CT)
        W("vector", CT, t_ev)
        OP("vector", lambda v, hcol=hcol: v.tensor_copy(dp1[:, hcol * 512:(hcol + 1) * 512], ps[7][:, 0:512]), inc=CV)
        if hcol == 0:
            W("tensor", CV, CV.n)
    v_dp1 = CV.n
    vf()

    ys_free = {2: bu_free[2], 3: bu_free[3]}
    v_init_ev = [0] * NTT
    s_init_ev = [0] * NTT
    for tt in range(NTT):
        par = tt % 2
        q = "sync"
        qc = CU
        if tt >= 2:
            W(q, CV, v_init_ev[tt - 2])
        u2_ev = dma(q, ub2[par], u_d[tt * 128:(tt + 1) * 128, :], qc)
        if tt == 0:
            W("tensor", CU, u_b2)
            W("tensor", CV, v_2c_done)
            W("tensor", CS, s_2c_done)
            W("tensor", CV, v_2b_done)
        for hcol in range(2):
            bank = 2 + hcol
            W("tensor", CV, ys_free[bank])
            for pc in range(NPC):
                for comp in range(2):
                    OP("tensor", lambda t, pc=pc, comp=comp, tt=tt, hcol=hcol, bank=bank:
                       t.matmul(ps[bank][:, 0:512], xs_pl[comp][pc][:, tt * 128:(tt + 1) * 128],
                                ct_pl[pc][comp][:, hcol * 512:(hcol + 1) * 512],
                                start=(pc == 0 and comp == 0), stop=False), inc=CT)
            t_ys = OP("tensor", lambda t, tt=tt, hcol=hcol, bank=bank:
                      t.matmul(ps[bank][:, 0:512], w_em[0:8, tt * 128:(tt + 1) * 128],
                               b2_sb[0:8, hcol * 512:(hcol + 1) * 512], start=False, stop=True), inc=CT)
            W("vector", qc, u2_ev)
            W("vector", CT, t_ys)
            OP("vector", lambda v, par=par, hcol=hcol, tt=tt:
               v.tensor_scalar(t1buf, ub2[par][:, hcol * 512:(hcol + 1) * 512],
                               stats[:, 2 * tt:2 * tt + 1], stats[:, 2 * tt + 1:2 * tt + 2],
                               ALU.add, ALU.mult), inc=CV)
            if tt == 0 and hcol == 0:
                W("vector", CV, v_dp1)
            OP("vector", lambda v, hcol=hcol:
               v.tensor_tensor(t1buf, t1buf, dp1[:, hcol * 512:(hcol + 1) * 512], ALU.mult), inc=CV)
            OP("vector", lambda v, par=par, hcol=hcol:
               v.tensor_tensor(t1buf, t1buf, ub2[par][:, hcol * 512:(hcol + 1) * 512], ALU.add), inc=CV)
            OP("vector", lambda v, tt=tt, hcol=hcol, bank=bank:
               v.tensor_tensor(out_acc[:, tt * 1024 + hcol * 512: tt * 1024 + (hcol + 1) * 512],
                               t1buf, ps[bank][:, 0:512], ALU.add), inc=CV)
            ys_free[bank] = CV.n
            v_t1_used = CV.n
        v_init_ev[tt] = CV.n
    t_ys_all = CT.n
    v_2d_done = CV.n

    if debug:
        W("sync", CV, v_phase1)
        dma("sync", dbg_w, w_rt, CSTD)
        dma("sync", dbg_xn, xn_bf, CSTD)
        W("sync", CV, v_2d_done)
        dma("sync", dbg_ct, bA[:, SCRB + 16384: SCRB + 24576], CSTD)
        for tt in range(NTT):
            dma("sync", dbg_acc[tt * 128:(tt + 1) * 128, :], out_acc[:, tt * 1024:(tt + 1) * 1024], CSTD)
        W("sync", CSTD, CSTD.n)

    w1q = [bsl(0, 8192), bsl(8192, 8192)]
    w2q = [bsl(16384, 8192), bsl(24576, 8192)]
    h_bf = [bsl(32768, 4096), bsl(36864, 4096)]
    cw_base = CW.n
    t_phase_end = [0] * 32
    t_g1_end = [0] * 32
    t_h_ev = []
    s_gl_ev = []
    s_tb_rdy = []
    t_o_ev = []
    v_o_ev = []
    t_o_last_of_g = []

    def gemm1(phi, tb):
        e, q = eorder[phi // 4], phi % 4
        par = phi % 2
        g = phi * 4 + tb
        hb = h_bf[g % 2]
        for ht in range(8):
            g_h = len(t_h_ev)
            bank = g_h % 2
            if g_h >= 2:
                W("tensor", CS, s_gl_ev[g_h - 2])
            elif phi == 0:
                W("tensor", CV, v_2d_done)
            for dc in range(NDC):
                OP("tensor", lambda t, dc=dc, ht=ht, tb=tb, par=par, bank=bank:
                   t.matmul(ps[bank][:, 0:512], w1q[par][:, dc * 1024 + ht * 128: dc * 1024 + (ht + 1) * 128],
                            xn_bf[:, dc * 2048 + tb * 512: dc * 2048 + (tb + 1) * 512],
                            start=(dc == 0), stop=(dc == NDC - 1)), inc=CT)
            t_h_ev.append(CT.n)
            W("scalar", CT, t_h_ev[-1])
            if g >= 2:
                W("scalar", CT, t_o_last_of_g[g - 2])
            s_gl_ev.append(OP("scalar", lambda s, ht=ht, e=e, q=q, bank=bank, hb=hb:
                              s.activation(hb[:, ht * 512:(ht + 1) * 512], ps[bank][:, 0:512], AF.Gelu,
                                           bias=b1c[:, e * 32 + q * 8 + ht: e * 32 + q * 8 + ht + 1], scale=1.0), inc=CS))
        OP("scalar", lambda s, hb=hb: s.activation(hb[:, 3584:4096], hb[:, 3584:4096], AF.Copy), inc=CS)
        s_tb_rdy.append(CS.n)

    def gemm2(phi, tb):
        e, q = eorder[phi // 4], phi % 4
        par = phi % 2
        g = phi * 4 + tb
        hb = h_bf[g % 2]
        W("tensor", CS, s_tb_rdy[g])
        for ts_ in range(4):
            for dc2 in range(2):
                j = len(t_o_ev)
                bank = 2 + j % 4
                if j >= 4:
                    W("tensor", CV, v_o_ev[j - 4])
                elif j == 0:
                    W("tensor", CV, v_2d_done)
                if debug and j == 0:
                    W("vector", CSTD, CSTD.n)
                for hc in range(8):
                    OP("tensor", lambda t, hc=hc, ts_=ts_, dc2=dc2, par=par, bank=bank, hb=hb:
                       t.matmul(ps[bank][:, 0:512], hb[:, hc * 512 + ts_ * 128: hc * 512 + (ts_ + 1) * 128],
                                w2q[par][:, hc * 1024 + dc2 * 512: hc * 1024 + (dc2 + 1) * 512],
                                start=(hc == 0), stop=(hc == 7)), inc=CT)
                t_o_ev.append(CT.n)
                tt = tb * 4 + ts_
                W("vector", CT, t_o_ev[-1])
                v_o_ev.append(OP("vector", lambda v, tt=tt, dc2=dc2, e=e, bank=bank:
                                 v.scalar_tensor_tensor(out_acc[:, tt * 1024 + dc2 * 512: tt * 1024 + (dc2 + 1) * 512],
                                                        ps[bank][:, 0:512], w_rt[:, tt * 8 + e: tt * 8 + e + 1],
                                                        out_acc[:, tt * 1024 + dc2 * 512: tt * 1024 + (dc2 + 1) * 512],
                                                        ALU.mult, ALU.add), inc=CV))
        t_o_last_of_g.append(t_o_ev[-1])

    for phi in range(32):
        e, q = eorder[phi // 4], phi % 4
        edr = dram_perm[e]
        par = phi % 2
        if phi < 2:
            W("gpsimd", CT, t_ys_all)
            W("gpsimd", CV, v_2b_done)
            if debug:
                W("gpsimd", CSTD, CSTD.n)
        else:
            W("gpsimd", CT, t_g1_end[phi - 2])
        dma("gpsimd", w1q[par].rearrange("p (dc h) -> p dc h", dc=8),
            w1_d[edr][:, q * 1024:(q + 1) * 1024].rearrange("(dc p) h -> p dc h", p=128), CW)
        if phi >= 2:
            W("gpsimd", CT, t_phase_end[phi - 2])
        dma("gpsimd", w2q[par].rearrange("p (hc d) -> p hc d", hc=8),
            w2_d[edr][q * 1024:(q + 1) * 1024, :].rearrange("(hc p) d -> p hc d", p=128), CW)
        W("tensor", CW, cw_base + 32 * phi + 16)
        gemm1(phi, 0)
        gemm1(phi, 1)
        W("tensor", CW, cw_base + 32 * (phi + 1))
        gemm2(phi, 0)
        gemm1(phi, 2)
        gemm2(phi, 1)
        gemm1(phi, 3)
        t_g1_end[phi] = CT.n
        gemm2(phi, 2)
        gemm2(phi, 3)
        t_phase_end[phi] = CT.n

    OP("vector", lambda v: v.tensor_copy(t1buf[:, 0:512], out_acc[:, 0:512]), inc=CV)
    OP("vector", lambda v: v.tensor_copy(t1buf[:, 0:512], out_acc[:, 512:1024]), inc=CV)
    OP("vector", lambda v: v.tensor_copy(t1buf[:, 0:512], out_acc[:, 1024:1536]), inc=CV)
    v_tail = CV.n
    for tt in range(NTT):
        j_last = ((31 * 4 + tt // 4) * 4 + (tt % 4)) * 2 + 1
        W("sync", CV, min(v_o_ev[j_last] + 3, v_tail))
        dma("sync", out_d[tt * 128:(tt + 1) * 128, :], out_acc[:, tt * 1024:(tt + 1) * 1024], CST)
    W("sync", CST, CST.n)

    with nc.Block() as block:
        @block.tensor
        def _(t):
            for fn in plan.streams["tensor"]:
                fn(t)

        @block.vector
        def _(v):
            for fn in plan.streams["vector"]:
                fn(v)

        @block.scalar
        def _(s):
            for fn in plan.streams["scalar"]:
                fn(s)

        @block.sync
        def _(s):
            for fn in plan.streams["sync"]:
                fn(s)

        @block.gpsimd
        def _(g):
            for fn in plan.streams["gpsimd"]:
                fn(g)

    return nc


_NC_CACHE = None


def kernel(L0_attn_u, L0_attn_Lambda_re, L0_attn_Lambda_im, L0_attn_B, L0_attn_C,
           L0_attn_D, L0_attn_log_step, L0_ffn_W1, L0_ffn_b1, L0_ffn_W2, L0_ffn_b2,
           L0_ffn_Wr, L0_ffn_br):
    global _NC_CACHE
    if _NC_CACHE is None:
        _NC_CACHE = build_program()
    nc = _NC_CACHE

    u = np.ascontiguousarray(np.asarray(L0_attn_u, dtype=np.float32))
    com = {
        "lre": np.ascontiguousarray(np.asarray(L0_attn_Lambda_re, dtype=np.float32)),
        "lim": np.ascontiguousarray(np.asarray(L0_attn_Lambda_im, dtype=np.float32)),
        "lstep": np.ascontiguousarray(np.asarray(L0_attn_log_step, dtype=np.float32)),
        "bmat": np.ascontiguousarray(np.asarray(L0_attn_B, dtype=np.float32)),
        "cmat": np.ascontiguousarray(np.asarray(L0_attn_C, dtype=np.float32)),
        "dvec": np.ascontiguousarray(np.asarray(L0_attn_D, dtype=np.float32)),
        "w1": np.ascontiguousarray(np.asarray(L0_ffn_W1, dtype=np.float32)),
        "b1": np.ascontiguousarray(np.asarray(L0_ffn_b1, dtype=np.float32)),
        "w2": np.ascontiguousarray(np.asarray(L0_ffn_W2, dtype=np.float32)),
        "b2": np.ascontiguousarray(np.asarray(L0_ffn_b2, dtype=np.float32)),
        "wr": np.ascontiguousarray(np.asarray(L0_ffn_Wr, dtype=np.float32)),
        "br": np.ascontiguousarray(np.asarray(L0_ffn_br, dtype=np.float32)),
    }
    in_maps = []
    for c in range(8):
        m = dict(com)
        m["u"] = np.ascontiguousarray(u[c])
        in_maps.append(m)
    res = run_bass_kernel_spmd(nc, in_maps, core_ids=list(range(8)))
    out = np.stack([np.asarray(res.results[c]["out"]) for c in range(8)], axis=0)
    return out.astype(np.float32)
